# version 11
# baseline (speedup 1.0000x reference)
import numpy as np
import ml_dtypes
import concourse.bass as bass
import concourse.mybir as mybir
from concourse.bass_utils import run_bass_kernel_spmd

F32 = mybir.dt.float32
BF16 = mybir.dt.bfloat16
I32 = mybir.dt.int32
AF = mybir.ActivationFunctionType
ALU = mybir.AluOpType
AX = mybir.AxisListType
BFNP = ml_dtypes.bfloat16

B, S, D = 4, 2048, 4096
T = 1024
HD = 128
NQ, NKV = 32, 8
WIN = 1024
SM = float(HD) ** -0.5
NEG = -1.0e30
EPS = 1e-5
VIS = int(S * 0.3)
NSLOT = 16
DT = D // 128
TT = T // 128

_CACHE = {}


class _Sem:
    def __init__(self, name):
        self.name = name
        self.n = 0
        self.h = None


class Res:
    def __init__(self):
        self.w = {}
        self.rs = {}

    def r_waits(self):
        return list(self.w.items())

    def w_waits(self):
        d = dict(self.w)
        for s, v in self.rs.items():
            d[s] = max(d.get(s, 0), v)
        return list(d.items())

    def did_read(self, sem, val):
        self.rs[sem] = max(self.rs.get(sem, 0), val)

    def did_write(self, sem, val):
        self.w = {sem: val}
        self.rs = {}


class KB:
    def __init__(self, nc, ctx):
        self.nc = nc
        self.q = {k: [] for k in "tvsgy"}
        self.sem = {}
        for name in ("sT", "sV", "sA", "sg", "sy"):
            s = _Sem(name)
            s.h = ctx.enter_context(nc.semaphore(name))
            self.sem[name] = s
        self.esem = {"t": self.sem["sT"], "v": self.sem["sV"], "s": self.sem["sA"],
                     "g": self.sem["sg"], "y": self.sem["sy"]}
        self.watermark = {k: {} for k in "tvsgy"}
        self.rx = {}
        self.dsem = {}
        self._ctx = ctx

    def R(self, name):
        if name not in self.rx:
            self.rx[name] = Res()
        return self.rx[name]

    def op(self, ek, fn, reads=(), writes=(), extra=(), dma=False,
           inc=True, record=True):
        sem = self.esem[ek]
        waits = {}
        for r in reads:
            for s, v in r.r_waits():
                waits[s] = max(waits.get(s, 0), v)
        for w in writes:
            for s, v in w.w_waits():
                waits[s] = max(waits.get(s, 0), v)
        for s, v in extra:
            waits[s] = max(waits.get(s, 0), v)
        wm = self.watermark[ek]
        wl = []
        for s, v in sorted(waits.items(), key=lambda kv: kv[0].name):
            if v <= 0:
                continue
            if wm.get(s.name, -1) < v:
                wl.append((s, v))
                wm[s.name] = v
        incs = []
        if inc:
            sem.n += 16 if dma else 1
            incs = [(sem, 16 if dma else 1)]
        self.q[ek].append((wl, fn, incs))
        if record:
            sv = (sem, sem.n)
            for r in reads:
                r.did_read(*sv)
            for w in writes:
                w.did_write(*sv)
        return (sem, sem.n)

    def chan(self, name):
        if name not in self.dsem:
            s = _Sem("c_" + name)
            s.h = self._ctx.enter_context(self.nc.semaphore("c_" + name))
            self.dsem[name] = s
        return self.dsem[name]

    def dma(self, ek, chan, dst, src, reads=(), writes=(), extra=(), **kw):
        sem = self.chan(chan)
        waits = {}
        for r in reads:
            for s, v in r.r_waits():
                waits[s] = max(waits.get(s, 0), v)
        for w in writes:
            for s, v in w.w_waits():
                waits[s] = max(waits.get(s, 0), v)
        for s, v in extra:
            waits[s] = max(waits.get(s, 0), v)
        wm = self.watermark[ek]
        wl = []
        for s, v in sorted(waits.items(), key=lambda kv: kv[0].name):
            if v <= 0:
                continue
            if wm.get(s.name, -1) < v:
                wl.append((s, v))
                wm[s.name] = v
        sem.n += 16
        self.q[ek].append((wl, (lambda e, d=dst, s=src: e.dma_start(out=d, in_=s, **kw)),
                           [(sem, 16)]))
        sv = (sem, sem.n)
        for r in reads:
            r.did_read(*sv)
        for w in writes:
            w.did_write(*sv)
        return sv

    def raw(self, ek, fn):
        self.q[ek].append(([], fn, []))

    def emit(self, blk):
        def runner(items):
            def f(eng):
                for waits, fn, incs in items:
                    for sem, val in waits:
                        eng.wait_ge(sem.h, val)
                    inst = fn(eng) if fn is not None else None
                    for sem, k in incs:
                        if inst is not None:
                            inst.then_inc(sem.h, k)
            return f
        blk.tensor(runner(self.q["t"]))
        blk.vector(runner(self.q["v"]))
        blk.scalar(runner(self.q["s"]))
        blk.gpsimd(runner(self.q["g"]))
        blk.sync(runner(self.q["y"]))


def build(upto=99, expose=(), sim_single=False, tap=None):
    nc = bass.Bass("TRN2", target_bir_lowering=False)

    def dram(name, shape, dt, kind="Internal", **kw):
        k = "ExternalOutput" if name in expose else kind
        return nc.dram_tensor(name, shape, dt, kind=k, **kw).ap()

    x_in = dram("x_own", [T, D], F32, "ExternalInput")
    wq_d = dram("wq", [D, D], F32, "ExternalInput")
    wk_d = dram("wk", [D, NKV * HD], F32, "ExternalInput")
    wv_d = dram("wv", [D, NKV * HD], F32, "ExternalInput")
    wo_d = dram("wo", [D, D], F32, "ExternalInput")
    w1_d = dram("w1", [3, D, D], F32, "ExternalInput")
    w2_d = dram("w2", [3, D, D], F32, "ExternalInput")
    b1_d = dram("b1a", [128, 3 * DT], F32, "ExternalInput")
    b2_d = dram("b2a", [128, 3 * DT], F32, "ExternalInput")
    gw_d = dram("gwa", [128, DT * 3], F32, "ExternalInput")
    gwl_d = dram("gwl", [1, 3], F32, "ExternalInput")
    gb_d = dram("gb", [1, 3], F32, "ExternalInput")
    cosq_d = dram("cosq", [64, T], F32, "ExternalInput")
    sinq_d = dram("sinq", [64, T], F32, "ExternalInput")
    cosk_d = dram("cosk", [64, T], F32, "ExternalInput")
    sink_d = dram("sink", [64, T], F32, "ExternalInput")
    mo1_d = dram("mo1", [128, TT * 1024], BF16, "ExternalInput")
    mo1d_d = dram("mo1d", [128, TT * 128], BF16, "ExternalInput")
    identf_d = dram("identf", [128, 128], F32, "ExternalInput")
    onesb_d = dram("onesb", [1, 128], BF16, "ExternalInput")
    onesf_d = dram("onesf", [1, 128], F32, "ExternalInput")
    nonce_d = dram("nonce", [1, 8], I32, "ExternalInput")
    sel_d = dram("sel", [1, 8], F32, "ExternalInput")
    out_d = dram("out", [T, D], F32, "ExternalOutput")

    qsc = dram("qsc", [NQ * HD, T], BF16)
    ohsc = dram("ohsc", [NQ * HD, T], BF16)
    x2sc = dram("x2sc", [T, D], F32)
    moesc = dram("moesc", [D, T], F32)
    kvsh = dram("kvsh", [2, 128, 16384], BF16, addr_space="Shared")
    stsh = dram("stsh", [2, 1, 16], F32, addr_space="Shared")
    flsh = dram("flsh", [2, 2, 1, 8], I32, addr_space="Shared")

    from contextlib import ExitStack
    ctx = ExitStack()
    sbt = lambda name, shape, dt: ctx.enter_context(nc.sbuf_tensor(name, shape, dt))
    pst = lambda name, shape, dt: ctx.enter_context(nc.psum_tensor(name, shape, dt))

    A = sbt("A", [128, 32768], BF16)[:]
    Bb = sbt("Bb", [128, 32768], BF16)[:]
    C = sbt("C", [128, 16384], BF16)[:]
    Dd = sbt("Dd", [128, 16384], BF16)[:]
    identf = sbt("identf_s", [128, 128], F32)[:]
    identb = sbt("identb_s", [128, 128], BF16)[:]
    onesb = sbt("onesb_s", [1, 128], BF16)[:]
    onesf = sbt("onesf_s", [1, 128], F32)[:]
    b1s = sbt("b1s", [128, 3 * DT], F32)[:]
    b2s = sbt("b2s", [128, 3 * DT], F32)[:]
    gws = sbt("gws", [128, DT * 3], BF16)[:]
    gwlb = sbt("gwlb", [1, 3], F32)[:]
    gbs = sbt("gbs", [1, 3], F32)[:]
    nons = sbt("nons", [1, 8], I32)[:]
    sels = sbt("sels", [1, 8], F32)[:]
    jsel = sbt("jselbuf", [1, 8], I32)[:]
    bsel = sbt("bsel", [128, 4 * DT], F32)[:]
    RW = sbt("RW", [128, 64], F32)[:]
    GL = sbt("GL", [128, TT * 3], F32)[:]
    GP = sbt("GP", [128, 64], F32)[:]
    PAY = sbt("PAY", [1, 16], F32)[:]
    PPAY = sbt("PPAY", [1, 16], F32)[:]
    FLG = sbt("FLG", [1, 8], I32)[:]
    SML = sbt("SML", [1, 64], F32)[:]
    CEPS = sbt("CEPS", [128, 1], F32)[:]

    Af = A.bitcast(F32)
    Bf = Bb.bitcast(F32)
    Cf = C.bitcast(F32)
    Df = Dd.bitcast(F32)

    W2 = [Bb[:, 4096 + i * 4096:4096 + (i + 1) * 4096] for i in range(2)]
    W5 = [Dd[:, 4096 + i * 4096:4096 + (i + 1) * 4096] for i in range(2)]
    W7 = [C[:, i * 4096:(i + 1) * 4096] for i in range(4)]
    ROW6 = Bf[0:1, 0:3072]
    XT = [Bf[:, i * 4096:(i + 1) * 4096] for i in range(2)]
    NRM = [Bf[:, 8192 + i * 4096:8192 + (i + 1) * 4096] for i in range(2)]
    SQ1 = Cf[:, 0:4096]
    cosq = Df[0:64, 0:1024]
    sinq = Df[0:64, 1024:2048]
    cosk = Df[0:64, 2048:3072]
    sink = Df[0:64, 3072:4096]
    RP = [Bf[0:64, i * 512:(i + 1) * 512] for i in range(4)]
    SC1 = [Bf[:, i * 1152:(i + 1) * 1152] for i in range(3)]
    SC1E = [Bf[:, 3456 + i * 1152:3456 + (i + 1) * 1152] for i in range(3)]
    QH = [Bb[:, 15360 + i * 1024:15360 + (i + 1) * 1024] for i in range(2)]
    OG = [Bb[:, 17408 + i * 128:17408 + (i + 1) * 128] for i in range(2)]
    PBF = [Bb[:, 17664 + i * 1152:17664 + (i + 1) * 1152] for i in range(3)]
    PTS = [Bb[:, 21120 + i * 1152:21120 + (i + 1) * 1152] for i in range(3)]
    MK1 = A[:, 0:8192]
    MK1D = A[:, 8192:9216]
    SG5 = [Df[:, i * 1024:(i + 1) * 1024] for i in range(2)]
    XT5 = Df[:, 0:4096]
    NRM5 = Df[:, 4096:8192]
    X2T = [Cf[:, 0:4096], Cf[:, 0:4096]]
    SQ5 = Cf[:, 4096:8192]
    STG = [Df[:, i * 512:(i + 1) * 512] for i in range(4)]
    WBC = [Df[:, 2048 + i * 1024:2048 + (i + 1) * 1024] for i in range(3)]
    MO8 = Df[:, 0:4096]
    MT8 = Df[:, 4096:8192]
    XR8 = [Cf[:, i * 4096:(i + 1) * 4096] for i in range(2)]
    SQ8 = Bf[:, 0:4096]
    NRM8 = Bf[:, 4096:8192]

    pM = [pst("pM0", [128, 1024], F32)[:], pst("pM1", [128, 1024], F32)[:]]
    pD = pst("pD", [128, 512], F32)[:]
    pT = [pst("pT0", [128, 1024], BF16)[:], pst("pT1", [128, 1024], BF16)[:]]
    pC = pst("pC", [128, 512], F32)[:]
    pCb = pC.bitcast(BF16)

    kb = KB(nc, ctx)
    R = kb.R
    sg, sy, sT, sV, sA = (kb.sem[n] for n in ("sg", "sy", "sT", "sV", "sA"))
    sX1 = _Sem("sX1"); sX1.h = ctx.enter_context(nc.semaphore("sX1"))
    sX2 = _Sem("sX2"); sX2.h = ctx.enter_context(nc.semaphore("sX2"))

    blk = ctx.enter_context(nc.Block())

    def ln_stats(rsrc, xa, sq_scratch, rsq, base):
        c = lambda i: RW[:, base + i:base + i + 1]
        r_c = R(f"rw{base}")
        kb.op("v", lambda e: e.reduce_sum(c(0), xa, axis=AX.X),
              reads=[rsrc], writes=[r_c])
        kb.op("s", lambda e: e.activation(sq_scratch, xa, AF.Square),
              reads=[rsrc], writes=[rsq])
        kb.op("v", lambda e: e.reduce_sum(c(1), sq_scratch, axis=AX.X),
              reads=[rsq], writes=[r_c])
        kb.op("v", lambda e: e.tensor_scalar_mul(c(2), c(0), 1.0 / D),
              reads=[r_c], writes=[r_c])
        kb.op("v", lambda e: e.tensor_scalar_mul(c(3), c(1), 1.0 / D),
              reads=[r_c], writes=[r_c])
        kb.op("v", lambda e: e.tensor_tensor(c(4), c(2), c(2), ALU.mult),
              reads=[r_c], writes=[r_c])
        kb.op("v", lambda e: e.tensor_tensor(c(5), c(3), c(4), ALU.subtract),
              reads=[r_c], writes=[r_c])
        kb.op("s", lambda e: e.activation(c(6), c(5), AF.Ln, bias=CEPS),
              reads=[r_c, R("const")], writes=[r_c])
        kb.op("s", lambda e: e.activation(c(7), c(6), AF.Exp, scale=-0.5),
              reads=[r_c], writes=[r_c])
        kb.op("v", lambda e: e.tensor_scalar_mul(c(8), c(2), -1.0),
              reads=[r_c], writes=[r_c])
        return c(8), c(7), r_c

    def transpose_block(src_f32, ncols, dst_bf, dcol0, t_off, rsrc, rdst):
        for g4 in range(0, ncols, 4):
            bank, rb = (pD, R("pD")) if (g4 // 4) % 2 == 0 else (pC, R("pCx"))
            cnt = min(4, ncols - g4)
            for k in range(cnt):
                last = k == cnt - 1
                kb.op("t", lambda e, b=bank, k=k, c0=(g4 + k) * 128: e.transpose(
                    b[:, k * 128:(k + 1) * 128], src_f32[:, c0:c0 + 128], identf),
                    reads=[rsrc, R("const")], writes=[rb],
                    inc=last, record=last)
            def cp(e, b=bank, g4=g4, cnt=cnt):
                dst = dst_bf.rearrange("p (dt t) -> p dt t", t=1024)[
                    :, dcol0 + g4:dcol0 + g4 + cnt, t_off:t_off + 128]
                return e.tensor_copy(
                    dst, b[:].rearrange("p (a b) -> p a b", b=128)[:, 0:cnt, :])
            kb.op("v", cp, reads=[rb], writes=[rdst])

    def stream_mm(w_dram, nslices, rhs_bf, r_rhs, evac, wbufs, wpre,
                  first_extra=()):
        for si in range(nslices):
            wbuf = si % 2
            rWb = R(f"wb{wbuf}")
            kb.dma("g", f"wb{wbuf}", wbufs[wbuf].rearrange("p (a c) -> p a c", c=128),
                   w_dram.rearrange("(ct p) c -> p ct c", p=128)
                   [:, :, si * 128:(si + 1) * 128],
                   writes=[rWb], extra=first_extra if si < 2 else ())
            pm, rpm = pM[si % 2], R(f"pM{si % 2}")
            for ct in range(DT):
                first, last = ct == 0, ct == DT - 1
                for tb in range(2):
                    lw = last and tb == 1
                    kb.op("t", lambda e, wbuf=wbuf, ct=ct, tb=tb, pm=pm, st_=first, sp_=last: e.matmul(
                        pm[:, tb * 512:(tb + 1) * 512],
                        wbufs[wbuf][:, ct * 128:(ct + 1) * 128],
                        rhs_bf[:, ct * 1024 + tb * 512: ct * 1024 + (tb + 1) * 512],
                        start=st_, stop=sp_),
                        reads=[rWb, r_rhs] if (first or lw) else [],
                        writes=[rpm] if (first or lw) else [],
                        inc=lw, record=lw)
            evac(si, pm, rpm)

    def dma3(chan, dst, srcs, reads=(), writes=(), extra=(), **kw):
        sem = kb.chan(chan)
        waits = {}
        for r in reads:
            for s, v in r.r_waits():
                waits[s] = max(waits.get(s, 0), v)
        for w in writes:
            for s, v in w.w_waits():
                waits[s] = max(waits.get(s, 0), v)
        for s, v in extra:
            waits[s] = max(waits.get(s, 0), v)
        wm = kb.watermark["g"]
        wl = []
        for s, v in sorted(waits.items(), key=lambda kv: kv[0].name):
            if v <= 0:
                continue
            if wm.get(s.name, -1) < v:
                wl.append((s, v))
                wm[s.name] = v
        sem.n += 16

        def f2(e, e_, dst=dst, srcs=srcs, kw=kw):
            return e.dma_start(out=dst, in_=srcs[e_], **kw)
        f2._sw3 = True
        kb.q["g"].append((wl, f2, [(sem, 16)]))
        sv = (sem, sem.n)
        for r in reads:
            r.did_read(*sv)
        for w in writes:
            w.did_write(*sv)
        return sv

    _gsw = [0]

    def g_switch_region(cell, rjs_waits, emit_fn):
        wm = kb.watermark["g"]
        wl = []
        for s, v in sorted(rjs_waits, key=lambda kv: kv[0].name):
            if v > 0 and wm.get(s.name, -1) < v:
                wl.append((s, v))
                wm[s.name] = v
        mark = len(kb.q["g"])
        emit_fn()
        items = kb.q["g"][mark:]
        del kb.q["g"][mark:]

        _gsw[0] += 1
        nm = f"gsw{_gsw[0]}"

        def fn(e, items=items, cell=cell, nm=nm):
            def arm(e_):
                for waits, f2, incs in items:
                    for sem, val in waits:
                        e.wait_ge(sem.h, val)
                    inst = f2(e, e_) if getattr(f2, "_sw3", False) else f2(e)
                    for sem, k in incs:
                        if inst is not None:
                            inst.then_inc(sem.h, k)
            with e.register(nm) as jr:
                e.reg_load(jr, cell)
                with e.If_eq(jr, 0):
                    arm(0)
                with e.Else():
                    with e.If_eq(jr, 1):
                        arm(1)
                    with e.Else():
                        arm(2)
            return None

        kb.q["g"].append((wl, fn, []))

    def stream_mm_sw(w_dram3, nslices, rhs_bf, r_rhs, evac, wbufs,
                     first_extra=()):
        nb = len(wbufs)

        def wdma(si):
            wbuf = si % nb
            rWb = R(f"wb{wbuf}")
            dma3(f"wb{wbuf}",
                 wbufs[wbuf].rearrange("p (a c) -> p a c", c=128),
                 [w_dram3[e_].rearrange("(ct p) c -> p ct c", p=128)
                  [:, :, si * 128:(si + 1) * 128] for e_ in range(3)],
                 reads=[R("jsel")], writes=[rWb],
                 extra=first_extra if si < nb else ())

        wdma(0)
        wdma(1)
        for si in range(nslices):
            if si + 2 < nslices:
                wdma(si + 2)
            wbuf = si % nb
            rWb = R(f"wb{wbuf}")
            pm, rpm = pM[si % 2], R(f"pM{si % 2}")
            for ct in range(DT):
                first, last = ct == 0, ct == DT - 1
                for tb in range(2):
                    lw = last and tb == 1
                    kb.op("t", lambda e, wbuf=wbuf, ct=ct, tb=tb, pm=pm, st_=first, sp_=last: e.matmul(
                        pm[:, tb * 512:(tb + 1) * 512],
                        wbufs[wbuf][:, ct * 128:(ct + 1) * 128],
                        rhs_bf[:, ct * 1024 + tb * 512: ct * 1024 + (tb + 1) * 512],
                        start=st_, stop=sp_),
                        reads=[rWb, r_rhs] if (first or lw) else [],
                        writes=[rpm] if (first or lw) else [],
                        inc=lw, record=lw)
            evac(si, pm, rpm)

    kb.op("v", lambda e: e.memset(CEPS, EPS), writes=[R("const")])
    for dst, src in ((identf, identf_d), (onesb, onesb_d), (onesf, onesf_d),
                     (b1s, b1_d), (b2s, b2_d), (gwlb, gwl_d),
                     (gbs, gb_d), (nons, nonce_d), (sels, sel_d)):
        kb.dma("y", "cst", dst, src, writes=[R("const")])
    kb.dma("g", "cstg", gws, gw_d, writes=[R("const")])
    rTAB = R("tables")
    for dst, src in ((cosq, cosq_d), (sinq, sinq_d), (cosk, cosk_d), (sink, sink_d)):
        kb.dma("y", "tab", dst, src, writes=[rTAB])

    h1 = A
    rH1 = R("h1")
    for tt in range(TT):
        bi = tt % 2
        rxt = R(f"XT{bi}")
        kb.dma("y", f"xt{bi}", XT[bi], x_in[tt * 128:(tt + 1) * 128, :], writes=[rxt])
        negm, rstd, r_c = ln_stats(rxt, XT[bi], SQ1, R("SQ1"), 0)
        rnrm = R(f"NRM{bi}")
        kb.op("v", lambda e, bi=bi, nm=negm, rs=rstd: e.tensor_scalar(
            NRM[bi], XT[bi], nm, rs, ALU.add, ALU.mult),
            reads=[rxt, r_c], writes=[rnrm])
        for g8 in range(4):
            transpose_block(NRM[bi][:, g8 * 1024:(g8 + 1) * 1024], 8,
                            h1, g8 * 8, tt * 128, rnrm, rH1)

    kall, vall = C, Dd
    rKA, rVA = R("kall"), R("vall")
    rQsc = R("qsc")

    if upto >= 2:
        def rope_evac(pp, rpp, cosT, sinT, dlo, dhi, tb, rdst):
            cc = cosT[:, tb * 512:(tb + 1) * 512]
            ss = sinT[:, tb * 512:(tb + 1) * 512]
            rrp = R("RP")
            kb.op("v", lambda e, pp=pp, cc=cc: e.tensor_tensor(
                RP[0], pp[0:64, :], cc, ALU.mult), reads=[rpp, rTAB], writes=[rrp])
            kb.op("v", lambda e, pp=pp, ss=ss: e.tensor_tensor(
                RP[1], pp[64:128, :], ss, ALU.mult), reads=[rpp, rTAB], writes=[rrp])
            kb.op("v", lambda e, dlo=dlo: e.tensor_tensor(
                dlo, RP[0], RP[1], ALU.subtract), reads=[rrp], writes=[rdst])
            kb.op("v", lambda e, pp=pp, ss=ss: e.tensor_tensor(
                RP[2], pp[0:64, :], ss, ALU.mult), reads=[rpp, rTAB], writes=[rrp])
            kb.op("v", lambda e, pp=pp, cc=cc: e.tensor_tensor(
                RP[3], pp[64:128, :], cc, ALU.mult), reads=[rpp, rTAB], writes=[rrp])
            kb.op("v", lambda e, dhi=dhi: e.tensor_tensor(
                dhi, RP[2], RP[3], ALU.add), reads=[rrp, rpp], writes=[rdst])

        p1w = []
        for rn in ("XT0", "XT1", "NRM0", "NRM1"):
            p1w += R(rn).w_waits()
        qkfirst = 0
        for kind, nheads, wsrc in (("q", NQ, wq_d), ("k", NKV, wk_d)):
            for head in range(nheads):
                wbuf = head % 2
                rWb = R(f"wb{wbuf}")
                kb.dma("g", f"wb{wbuf}", W2[wbuf].rearrange("p (a c) -> p a c", c=128),
                       wsrc.rearrange("(ct p) c -> p ct c", p=128)
                       [:, :, head * 128:(head + 1) * 128], writes=[rWb],
                       extra=p1w if qkfirst < 2 else ())
                qkfirst += 1
                pp, rpp = pM[head % 2], R(f"pM{head % 2}")
                for ct in range(DT):
                    first, last = ct == 0, ct == DT - 1
                    for tb in range(2):
                        lw = last and tb == 1
                        kb.op("t", lambda e, wbuf=wbuf, ct=ct, tb=tb, pp=pp, st_=first, sp_=last: e.matmul(
                            pp[:, tb * 512:(tb + 1) * 512],
                            W2[wbuf][:, ct * 128:(ct + 1) * 128],
                            h1[:, ct * 1024 + tb * 512: ct * 1024 + (tb + 1) * 512],
                            start=st_, stop=sp_),
                            reads=[rWb, rH1] if (first or lw) else [],
                            writes=[rpp] if (first or lw) else [],
                            inc=lw, record=lw)
                for tb in range(2):
                    if kind == "q":
                        rqh = R(f"QH{head % 2}")
                        rope_evac(pp[:, tb * 512:(tb + 1) * 512], rpp, cosq, sinq,
                                  QH[head % 2][0:64, tb * 512:(tb + 1) * 512],
                                  QH[head % 2][64:128, tb * 512:(tb + 1) * 512],
                                  tb, rqh)
                    else:
                        ko = head * NSLOT * 128 + 8 * 128 + tb * 512
                        rope_evac(pp[:, tb * 512:(tb + 1) * 512], rpp, cosk, sink,
                                  kall[0:64, ko:ko + 512], kall[64:128, ko:ko + 512],
                                  tb, rKA)
                if kind == "q":
                    kb.dma("y", f"qsc{head % 2}", qsc[head * 128:(head + 1) * 128, :], QH[head % 2],
                           reads=[R(f"QH{head % 2}")], writes=[rQsc])
        for vs in range(8):
            wbuf = vs % 2
            rWb = R(f"wb{wbuf}")
            kb.dma("g", f"wb{wbuf}", W2[wbuf].rearrange("p (a c) -> p a c", c=128),
                   wv_d.rearrange("(ct p) c -> p ct c", p=128)
                   [:, :, vs * 128:(vs + 1) * 128], writes=[rWb])
            for jt in range(TT):
                pp, rpp = (pC, R("pCx")) if jt % 2 == 0 else (pD, R("pD"))
                for ct in range(DT):
                    first, last = ct == 0, ct == DT - 1
                    kb.op("t", lambda e, wbuf=wbuf, ct=ct, jt=jt, pp=pp, st_=first, sp_=last: e.matmul(
                        pp[:, 0:128],
                        h1[:, ct * 1024 + jt * 128: ct * 1024 + (jt + 1) * 128],
                        W2[wbuf][:, ct * 128:(ct + 1) * 128],
                        start=st_, stop=sp_),
                        reads=[rWb, rH1] if (first or last) else [],
                        writes=[rpp] if (first or last) else [],
                        inc=last, record=last)
                vo = (8 + jt) * 1024 + vs * 128
                kb.op("v", lambda e, pp=pp, vo=vo: e.tensor_copy(
                    vall[:, vo:vo + 128], pp[:, 0:128]),
                    reads=[rpp], writes=[rVA])

    if upto >= 3:
        exw = {}
        for r in (rKA, rVA, rTAB, R("QH0"), R("QH1")):
            for s, v in r.w_waits():
                exw[s] = max(exw.get(s, 0), v)
        exw_list = sorted(exw.items(), key=lambda kv: kv[0].name)

        def f_exchange(g):
            for s, v in exw_list:
                g.wait_ge(s.h, v)
            with g.register("fva1") as fva, g.register("fvb1") as fvb, \
                    g.register("tgt1") as tgt, g.register("wna1") as wna, \
                    g.register("wnb1") as wnb:
                for arm in nc.switch_core_id(engines=[g], n=128):
                    me = 0 if sim_single else int(arm) % 2
                    other = 0 if sim_single else 1 - me
                    g.dma_start(
                        out=kvsh[me].rearrange("p (h j) -> p h j", j=1024)[:, 0:8, :],
                        in_=kall.rearrange("p (h s) -> p h s", s=NSLOT * 128)
                        [:, :, 8 * 128:16 * 128]).then_inc(sX1.h, 16)
                    g.dma_start(out=kvsh[me, :, 8192:16384],
                                in_=vall[:, 8 * 1024:16 * 1024]).then_inc(sX1.h, 16)
                    g.wait_ge(sX1.h, 32)
                    g.dma_start(out=flsh[0, me], in_=nons).then_inc(sX1.h, 16)
                    g.wait_ge(sX1.h, 48)
                    g.reg_load(wna, nons[0:1, 0:1])
                    g.reg_load(wnb, nons[0:1, 1:2])
                    g.reg_mov(fva, 1)
                    g.reg_mov(tgt, 64)
                    with g.While(fva):
                        g.dma_start(out=FLG, in_=flsh[0, other]).then_inc(sX1.h, 16)
                        g.wait_ge(sX1.h, tgt)
                        g.reg_add(tgt, tgt, 16)
                        g.reg_load(fva, FLG[0:1, 0:1])
                        g.reg_load(fvb, FLG[0:1, 1:2])
                        g.reg_alu(fva, fva, wna, op=ALU.subtract)
                        g.reg_alu(fvb, fvb, wnb, op=ALU.subtract)
                        g.reg_alu(fva, fva, fvb, op=ALU.bitwise_or)
                    g.dma_start(
                        out=kall.rearrange("p (h s) -> p h s", s=NSLOT * 128)
                        [:, :, 0:8 * 128],
                        in_=kvsh[other].rearrange("p (h j) -> p h j", j=1024)[:, 0:8, :]
                    ).then_inc(kb.chan("kvex").h, 16)
                    g.dma_start(out=vall[:, 0:8 * 1024],
                                in_=kvsh[other, :, 8192:16384]).then_inc(kb.chan("kvex").h, 16)

        kvex = kb.chan("kvex")

        if sim_single:
            kb.dma("g", "kvex",
                   kall.rearrange("p (h s) -> p h s", s=NSLOT * 128)[:, :, 0:8 * 128],
                   kall.rearrange("p (h s) -> p h s", s=NSLOT * 128)[:, :, 8 * 128:16 * 128],
                   extra=exw_list)
            kb.dma("g", "kvex", vall[:, 0:8 * 1024], vall[:, 8 * 1024:16 * 1024],
                   extra=exw_list)
            rKA.did_write(kvex, kvex.n)
            rVA.did_write(kvex, kvex.n)
        else:
            kb.raw("g", f_exchange)
            kvex.n += 32
            rKA.did_write(kvex, kvex.n)
            rVA.did_write(kvex, kvex.n)
        kb.watermark

    rMK = rH1
    if upto >= 4:
        kb.dma("y", "msk", MK1, mo1_d, writes=[rMK])
        kb.dma("y", "msk", MK1D, mo1d_d, writes=[rMK])
        kb.op("v", lambda e: e.tensor_copy(identb, identf),
              reads=[R("const")], writes=[R("constb")])
        qw = [(kb.chan("qsc0"), kb.chan("qsc0").n), (kb.chan("qsc1"), kb.chan("qsc1").n)]
        kb.dma("y", "qh0", QH[0], qsc[0:128, :], reads=[rQsc], writes=[R("QH0")],
               extra=qw)
        rOHs = R("ohsc")

        def p4_scores(idx):
            hq, ta = idx // 8, idx % 8
            h = hq // 4
            qb = hq % 2
            rqh = R(f"QH{qb}")
            if ta == 0 and hq + 1 < NQ:
                kb.dma("y", f"qh{(hq + 1) % 2}", QH[(hq + 1) % 2],
                       qsc[(hq + 1) * 128:(hq + 2) * 128, :],
                       reads=[rQsc], writes=[R(f"QH{(hq + 1) % 2}")], extra=qw)
            pm, rpm = pM[idx % 2], R(f"pM{idx % 2}")
            k0 = h * NSLOT * 128 + ta * 128
            for pc in range(2):
                kb.op("t", lambda e, pm=pm, qb=qb, ta=ta, k0=k0, pc=pc:
                      e.matmul(pm[:, pc * 512:(pc + 1) * 512],
                               QH[qb][:, ta * 128:(ta + 1) * 128],
                               kall[:, k0 + pc * 512:k0 + (pc + 1) * 512],
                               start=True, stop=True),
                      reads=[rqh, rKA], writes=[rpm],
                      inc=(pc == 1), record=(pc == 1))
            dreg = pD[:, (idx % 4) * 128:(idx % 4 + 1) * 128]
            rdr = R(f"pD{idx % 4}")
            kb.op("t", lambda e, dreg=dreg, qb=qb, ta=ta, k0=k0: e.matmul(
                dreg, QH[qb][:, ta * 128:(ta + 1) * 128],
                kall[:, k0 + 1024:k0 + 1152], start=True, stop=True),
                reads=[rqh, rKA], writes=[rdr])

        def p4_A(idx):
            hq, ta = idx // 8, idx % 8
            bi = idx % 3
            pm, rpm = pM[idx % 2], R(f"pM{idx % 2}")
            dreg = pD[:, (idx % 4) * 128:(idx % 4 + 1) * 128]
            rdr = R(f"pD{idx % 4}")
            rsc = R(f"SC1_{bi}")
            kb.op("v", lambda e, bi=bi, pm=pm, ta=ta: e.scalar_tensor_tensor(
                SC1[bi][:, 0:1024], pm, 1.0, MK1[:, ta * 1024:(ta + 1) * 1024],
                ALU.mult, ALU.add), reads=[rpm, rMK], writes=[rsc])
            kb.op("v", lambda e, bi=bi, dreg=dreg, ta=ta: e.scalar_tensor_tensor(
                SC1[bi][:, 1024:1152], dreg, 1.0,
                MK1D[:, ta * 128:(ta + 1) * 128], ALU.mult, ALU.add),
                reads=[rdr, rMK], writes=[rsc])
            cc = lambda i, idx=idx: RW[:, 16 + (idx % 4) * 4 + i:
                                       16 + (idx % 4) * 4 + i + 1]
            rcol = R(f"acol{idx % 4}")
            kb.op("v", lambda e, bi=bi, cc=cc: e.reduce_max(
                cc(0), SC1[bi], axis=AX.X, negate=True),
                reads=[rsc], writes=[rcol])
            rse = R(f"SC1E_{bi}")
            kb.op("s", lambda e, bi=bi, cc=cc: e.activation(
                SC1E[bi], SC1[bi], AF.Exp, bias=cc(0), accum_out=cc(1)),
                reads=[rsc, rcol], writes=[rse, rcol])

        def p4_B(idx):
            bi = idx % 3
            cc = lambda i, idx=idx: RW[:, 16 + (idx % 4) * 4 + i:
                                       16 + (idx % 4) * 4 + i + 1]
            rcol = R(f"acol{idx % 4}")
            rse = R(f"SC1E_{bi}")
            kb.op("v", lambda e, cc=cc: e.reciprocal(cc(2), cc(1)),
                  reads=[rcol], writes=[rcol])
            rpb = R(f"PBF_{bi}")
            kb.op("s", lambda e, bi=bi, cc=cc: e.mul(
                PBF[bi], SC1E[bi], cc(2)),
                reads=[rse, rcol], writes=[rpb])
            rpT = R(f"pT{idx % 2}")
            for sl in range(8):
                kb.op("t", lambda e, bi=bi, sl=sl, b2=idx % 2: e.transpose(
                    pT[b2][:, sl * 128:(sl + 1) * 128],
                    PBF[bi][:, sl * 128:(sl + 1) * 128], identb),
                    reads=[rpb, R("constb")], writes=[rpT],
                    inc=(sl == 7), record=(sl == 7))
            rpd = R(f"pCd{idx % 2}")
            kb.op("t", lambda e, bi=bi, b2=idx % 2: e.transpose(
                pCb[:, 512 + b2 * 128:512 + (b2 + 1) * 128],
                PBF[bi][:, 1024:1152], identb),
                reads=[rpb], writes=[rpd])

        def p4_C(idx):
            hq, ta = idx // 8, idx % 8
            h = hq // 4
            bi = idx % 3
            b2 = idx % 2
            rpT = R(f"pT{b2}")
            rpd = R(f"pCd{b2}")
            rpts = R(f"PTS_{bi}")
            kb.op("s", lambda e, bi=bi, b2=b2: e.copy(PTS[bi][:, 0:1024],
                                                      pT[b2]),
                  reads=[rpT], writes=[rpts])
            kb.op("v", lambda e, bi=bi, b2=b2: e.tensor_copy(
                PTS[bi][:, 1024:1152],
                pCb[:, 512 + b2 * 128:512 + (b2 + 1) * 128]),
                reads=[rpd], writes=[rpts])
            pcr = pC[:, b2 * 128:(b2 + 1) * 128]
            rpc = R(f"pC{b2}")
            for s_ in range(9):
                kb.op("t", lambda e, s_=s_, ta=ta, h=h, bi=bi, pcr=pcr: e.matmul(
                    pcr, vall[:, (ta + s_) * 1024 + h * 128:
                              (ta + s_) * 1024 + (h + 1) * 128],
                    PTS[bi][:, s_ * 128:(s_ + 1) * 128],
                    start=(s_ == 0), stop=(s_ == 8)),
                    reads=[rpts, rVA] if s_ in (0, 8) else [],
                    writes=[rpc] if s_ in (0, 8) else [],
                    inc=(s_ == 8), record=(s_ == 8))
            rog = R(f"OG{idx % 2}")
            kb.op("s", lambda e, pcr=pcr, og=OG[idx % 2]: e.copy(og, pcr),
                  reads=[rpc], writes=[rog])
            kb.dma("y", f"ohs{idx % 2}",
                   ohsc[hq * 128:(hq + 1) * 128, ta * 128:(ta + 1) * 128],
                   OG[idx % 2], reads=[rog], writes=[rOHs])

        NTIL = NQ * 8
        p4_scores(0)
        for idx in range(NTIL):
            if idx + 1 < NTIL:
                p4_scores(idx + 1)
            p4_A(idx)
            if idx >= 2:
                p4_C(idx - 2)
            if idx >= 1:
                p4_B(idx - 1)
        p4_C(NTIL - 2)
        p4_B(NTIL - 1)
        p4_C(NTIL - 1)

    rATP = R("attnp")
    rHF = rMK
    rX2 = R("x2sc")
    if upto >= 5:
        ohw = [(kb.chan("ohs0"), kb.chan("ohs0").n),
               (kb.chan("ohs1"), kb.chan("ohs1").n)]
        kb.dma("y", "ohr", A.rearrange("p (ht t) -> p ht t", t=1024),
               ohsc.rearrange("(ht p) t -> p ht t", p=128),
               reads=[rOHs], writes=[rMK], extra=ohw)
        attnp = Bb

        def wo_evac(si, pm, rpm):
            sg5, rsg = SG5[si % 2], R(f"SG5{si % 2}")
            kb.op("v", lambda e, pm=pm, sg5=sg5: e.tensor_copy(sg5, pm),
                  reads=[rpm], writes=[rsg])
            for g4 in range(2):
                bank, rb = (pD, R("pD")) if g4 == 0 else (pC, R("pCx"))
                for k in range(4):
                    last = k == 3
                    kb.op("t", lambda e, b=bank, sg5=sg5, g4=g4, k=k: e.transpose(
                        b[:, k * 128:(k + 1) * 128],
                        sg5[:, (g4 * 4 + k) * 128:(g4 * 4 + k + 1) * 128], identf),
                        reads=[rsg, R("const")], writes=[rb], inc=last, record=last)
                def cp5(e, b=bank, g4=g4, si=si):
                    dst = attnp.rearrange("p (tt d) -> p tt d", d=D)[
                        :, g4 * 4:g4 * 4 + 4, si * 128:(si + 1) * 128]
                    return e.tensor_copy(
                        dst, b[:].rearrange("p (a b) -> p a b", b=128))
                kb.op("v", cp5, reads=[rb], writes=[rATP])

        stream_mm(wo_d, DT, A, rMK, wo_evac, W5, "W5",
                  first_extra=rVA.w_waits())

        hffn = A
        for tt in range(TT):
            atile = attnp[:, tt * D:(tt + 1) * D]
            negm, rstd, r_c = ln_stats(rATP, atile, SQ5, R("SQ5"), 0)
            rn5 = R("NRM5")
            kb.op("v", lambda e, atile=atile, nm=negm, rs=rstd: e.tensor_scalar(
                NRM5, atile, nm, rs, ALU.add, ALU.mult),
                reads=[rATP, r_c], writes=[rn5])
            rxt5 = R("XT5")
            kb.dma("y", "xt5", XT5, x_in[tt * 128:(tt + 1) * 128, :], writes=[rxt5],
                   extra=(R("SG50").w_waits() + R("SG51").w_waits()))
            x2t, rx2t = X2T[0], R("X2T0")
            kb.op("v", lambda e, x2t=x2t: e.tensor_tensor(x2t, XT5, NRM5, ALU.add),
                  reads=[rxt5, rn5], writes=[rx2t])
            kb.dma("y", "x2w", x2sc[tt * 128:(tt + 1) * 128, :], x2t,
                   reads=[rx2t], writes=[rX2])
            negm2, rstd2, r_c2 = ln_stats(rx2t, x2t, SQ5, R("SQ5"), 16)
            kb.op("v", lambda e, x2t=x2t, nm=negm2, rs=rstd2: e.tensor_scalar(
                NRM5, x2t, nm, rs, ALU.add, ALU.mult),
                reads=[rx2t, r_c2], writes=[rn5])
            for g8 in range(4):
                transpose_block(NRM5[:, g8 * 1024:(g8 + 1) * 1024], 8,
                                hffn, g8 * 8, tt * 128, rn5, rHF)

    rWBC = R("wbc")
    if upto >= 6:
        hffn = A
        rGPa, rGPb = R("gpa"), R("gpb")
        rsA_, rsB_ = R("sumA"), R("sumB")
        for ct in range(DT):
            kb.op("v", lambda e, ct=ct: e.tensor_reduce(
                GP[:, ct:ct + 1], hffn[:, ct * 1024:ct * 1024 + VIS],
                axis=AX.X, op=ALU.add, apply_absolute_value=True),
                reads=[rHF], writes=[rGPa])
        kb.op("v", lambda e: e.reduce_sum(RW[:, 48:49], GP[:, 0:DT], axis=AX.X),
              reads=[rGPa], writes=[rsA_])
        for ct in range(DT):
            kb.op("v", lambda e, ct=ct: e.tensor_reduce(
                GP[:, ct:ct + 1], hffn[:, ct * 1024 + VIS:(ct + 1) * 1024],
                axis=AX.X, op=ALU.add, apply_absolute_value=True),
                reads=[rHF], writes=[rGPb], extra=rsA_.w_waits())
        kb.op("v", lambda e: e.reduce_sum(RW[:, 49:50], GP[:, 0:DT], axis=AX.X),
              reads=[rGPb], writes=[rsB_])
        kb.op("t", lambda e: e.transpose(pD[0:1, 0:128], RW[:, 48:49], identf),
              reads=[rsA_, R("const")], writes=[R("pD")])
        kb.op("t", lambda e: e.transpose(pC[0:1, 0:128], RW[:, 49:50], identf),
              reads=[rsB_], writes=[R("pCx")])
        rpay = R("pay")
        kb.op("v", lambda e: e.memset(PAY, 0.0), writes=[rpay])
        kb.op("v", lambda e: e.reduce_sum(PAY[0:1, 0:1], pD[0:1, 0:128], axis=AX.X),
              reads=[R("pD")], writes=[rpay])
        kb.op("v", lambda e: e.reduce_sum(PAY[0:1, 1:2], pC[0:1, 0:128], axis=AX.X),
              reads=[R("pCx")], writes=[rpay])
        rGL = R("glraw")
        for tt in range(TT):
            pb, rpb = (pC, R("pCx")) if tt % 2 == 0 else (pD, R("pD"))
            for ct in range(DT):
                first, last = ct == 0, ct == DT - 1
                kb.op("t", lambda e, ct=ct, tt=tt, pb=pb, st_=first, sp_=last: e.matmul(
                    pb[:, 0:3],
                    hffn[:, ct * 1024 + tt * 128: ct * 1024 + (tt + 1) * 128],
                    gws[:, ct * 3:(ct + 1) * 3], start=st_, stop=sp_),
                    reads=[rHF, R("const")] if (first or last) else [],
                    writes=[rpb] if (first or last) else [],
                    inc=last, record=last)
            kb.op("v", lambda e, pb=pb, tt=tt: e.tensor_copy(
                GL[:, tt * 3:(tt + 1) * 3], pb[:, 0:3]),
                reads=[rpb], writes=[rGL])
        kb.op("v", lambda e: e.tensor_copy(PAY[0:1, 2:5], GL[0:1, 0:3]),
              reads=[rGL], writes=[rpay])

        payw = sorted(rpay.w_waits(), key=lambda kv: kv[0].name)

        def f_exch2(g):
            for s, v in payw:
                g.wait_ge(s.h, v)
            with g.register("fva2") as fva, g.register("fvb2") as fvb, \
                    g.register("tgt2") as tgt, g.register("wna2") as wna, \
                    g.register("wnb2") as wnb:
                for arm in nc.switch_core_id(engines=[g], n=128):
                    me = 0 if sim_single else int(arm) % 2
                    other = 0 if sim_single else 1 - me
                    g.dma_start(out=stsh[me], in_=PAY).then_inc(sX2.h, 16)
                    g.wait_ge(sX2.h, 16)
                    g.dma_start(out=flsh[1, me], in_=nons).then_inc(sX2.h, 16)
                    g.wait_ge(sX2.h, 32)
                    g.reg_load(wna, nons[0:1, 2:3])
                    g.reg_load(wnb, nons[0:1, 3:4])
                    g.reg_mov(fva, 1)
                    g.reg_mov(tgt, 48)
                    with g.While(fva):
                        g.dma_start(out=FLG, in_=flsh[1, other]).then_inc(sX2.h, 16)
                        g.wait_ge(sX2.h, tgt)
                        g.reg_add(tgt, tgt, 16)
                        g.reg_load(fva, FLG[0:1, 2:3])
                        g.reg_load(fvb, FLG[0:1, 3:4])
                        g.reg_alu(fva, fva, wna, op=ALU.subtract)
                        g.reg_alu(fvb, fvb, wnb, op=ALU.subtract)
                        g.reg_alu(fva, fva, fvb, op=ALU.bitwise_or)
                    g.dma_start(out=PPAY, in_=stsh[other]).then_inc(kb.chan("st2").h, 16)

        st2 = kb.chan("st2")
        if sim_single:
            rppy = R("ppay")
            kb.dma("g", "st2", PPAY, PAY, reads=[rpay], writes=[rppy], extra=payw)
        else:
            kb.raw("g", f_exch2)
            st2.n += 16
            rppy = R("ppay")
            rppy.did_write(st2, st2.n)
            rpay.did_read(st2, st2.n)

        sm_ = lambda i: SML[0:1, i:i + 1]
        rsm = R("sml")
        sap = lambda i: sels[0:1, i:i + 1]
        kb.op("v", lambda e: e.tensor_scalar_mul(sm_(0), PAY[0:1, 0:1], sap(0)),
              reads=[rpay, R("const")], writes=[rsm])
        kb.op("v", lambda e: e.scalar_tensor_tensor(
            sm_(0), PPAY[0:1, 0:1], sap(1), sm_(0), ALU.mult, ALU.add),
            reads=[rppy, rsm], writes=[rsm])
        kb.op("v", lambda e: e.tensor_scalar_mul(sm_(1), PAY[0:1, 0:1], sap(2)),
              reads=[rpay], writes=[rsm])
        kb.op("v", lambda e: e.scalar_tensor_tensor(
            sm_(1), PAY[0:1, 1:2], sap(3), sm_(1), ALU.mult, ALU.add),
            reads=[rpay, rsm], writes=[rsm])
        kb.op("v", lambda e: e.scalar_tensor_tensor(
            sm_(1), PPAY[0:1, 0:1], sap(4), sm_(1), ALU.mult, ALU.add),
            reads=[rppy, rsm], writes=[rsm])
        kb.op("v", lambda e: e.scalar_tensor_tensor(
            sm_(1), PPAY[0:1, 1:2], sap(5), sm_(1), ALU.mult, ALU.add),
            reads=[rppy, rsm], writes=[rsm])
        kb.op("v", lambda e: e.tensor_scalar(
            sm_(2), sm_(0), 1.0 / (VIS * D), 1e-8, ALU.mult, ALU.add),
            reads=[rsm], writes=[rsm])
        kb.op("v", lambda e: e.reciprocal(sm_(3), sm_(2)), reads=[rsm], writes=[rsm])
        kb.op("v", lambda e: e.tensor_scalar_mul(
            sm_(4), sm_(1), 1.0 / ((S - VIS) * D)), reads=[rsm], writes=[rsm])
        kb.op("v", lambda e: e.tensor_tensor(sm_(5), sm_(4), sm_(3), ALU.mult),
              reads=[rsm], writes=[rsm])
        kb.op("v", lambda e: e.tensor_scalar_mul(SML[0:1, 8:11], PAY[0:1, 2:5],
                                                 sap(6)), reads=[rpay], writes=[rsm])
        kb.op("v", lambda e: e.scalar_tensor_tensor(
            SML[0:1, 8:11], PPAY[0:1, 2:5], sap(7), SML[0:1, 8:11],
            ALU.mult, ALU.add), reads=[rppy, rsm], writes=[rsm])
        kb.op("v", lambda e: e.tensor_scalar_mul(SML[0:1, 12:15], gwlb, sm_(5)),
              reads=[R("const"), rsm], writes=[rsm])
        kb.op("v", lambda e: e.tensor_tensor(
            SML[0:1, 8:11], SML[0:1, 8:11], SML[0:1, 12:15], ALU.add),
            reads=[rsm], writes=[rsm])
        kb.op("v", lambda e: e.tensor_tensor(
            SML[0:1, 8:11], SML[0:1, 8:11], gbs, ALU.add),
            reads=[rsm, R("const")], writes=[rsm])
        kb.op("v", lambda e: e.tensor_reduce(sm_(16), SML[0:1, 8:11], axis=AX.X,
                                             op=ALU.max), reads=[rsm], writes=[rsm])
        kb.op("v", lambda e: e.tensor_scalar_mul(sm_(17), sm_(16), -1.0),
              reads=[rsm], writes=[rsm])
        kb.op("s", lambda e: e.activation(SML[0:1, 18:21], SML[0:1, 8:11], AF.Exp,
                                          bias=sm_(17), accum_out=sm_(21)),
              reads=[rsm], writes=[rsm])
        kb.op("v", lambda e: e.reciprocal(sm_(22), sm_(21)), reads=[rsm], writes=[rsm])
        kb.op("v", lambda e: e.tensor_scalar_mul(SML[0:1, 24:27], SML[0:1, 18:21],
                                                 sm_(22)), reads=[rsm], writes=[rsm])
        kb.op("v", lambda e: e.tensor_reduce(sm_(28), SML[0:1, 24:27], axis=AX.X,
                                             op=ALU.max), reads=[rsm], writes=[rsm])
        kb.op("v", lambda e: e.tensor_reduce(sm_(29), SML[0:1, 24:27], axis=AX.X,
                                             op=ALU.min), reads=[rsm], writes=[rsm])
        kb.op("v", lambda e: e.tensor_single_scalar(
            SML[0:1, 32:35], SML[0:1, 24:27], sm_(28), ALU.is_ge),
            reads=[rsm], writes=[rsm])
        kb.op("v", lambda e: e.tensor_single_scalar(
            SML[0:1, 36:39], SML[0:1, 24:27], sm_(29), ALU.is_le),
            reads=[rsm], writes=[rsm])
        kb.op("v", lambda e: e.tensor_tensor(
            SML[0:1, 40:43], SML[0:1, 32:35], SML[0:1, 36:39], ALU.add),
            reads=[rsm], writes=[rsm])
        kb.op("v", lambda e: e.tensor_scalar(
            SML[0:1, 44:47], SML[0:1, 40:43], -1.0, 1.0, ALU.mult, ALU.add),
            reads=[rsm], writes=[rsm])
        kb.op("v", lambda e: e.tensor_tensor(
            SML[0:1, 12:15], SML[0:1, 12:15], gbs, ALU.add),
            reads=[rsm, R("const")], writes=[rsm])
        kb.op("t", lambda e: e.matmul(pD[:, 0:3], onesf, SML[0:1, 12:15],
                                      start=True, stop=True),
              reads=[rsm, R("const")], writes=[R("pD")])
        rcorr = R("corrb")
        kb.op("v", lambda e: e.tensor_copy(GP[:, 32:35], pD[:, 0:3]),
              reads=[R("pD")], writes=[rcorr])
        rW12 = R("w12rows")
        rgc = R("gcol")
        for tt in range(TT):
            g3 = GP[:, 36:39]
            rg3 = R("gp3")
            kb.op("v", lambda e, tt=tt: e.tensor_tensor(
                g3, GL[:, tt * 3:(tt + 1) * 3], GP[:, 32:35], ALU.add),
                reads=[rGL, rcorr], writes=[rg3])
            kb.op("v", lambda e: e.tensor_reduce(RW[:, 52:53], g3, axis=AX.X,
                                                 op=ALU.max), reads=[rg3], writes=[rgc])
            kb.op("v", lambda e: e.tensor_scalar_mul(RW[:, 53:54], RW[:, 52:53], -1.0),
                  reads=[rgc], writes=[rgc])
            kb.op("s", lambda e: e.activation(GP[:, 40:43], g3, AF.Exp,
                                              bias=RW[:, 53:54],
                                              accum_out=RW[:, 54:55]),
                  reads=[rg3, rgc], writes=[rg3, rgc])
            kb.op("v", lambda e: e.reciprocal(RW[:, 55:56], RW[:, 54:55]),
                  reads=[rgc], writes=[rgc])
            kb.op("v", lambda e: e.tensor_scalar_mul(GP[:, 44:47], GP[:, 40:43],
                                                     RW[:, 55:56]),
                  reads=[rg3, rgc], writes=[rg3])
            kb.op("v", lambda e: e.tensor_reduce(RW[:, 56:57], GP[:, 44:47],
                                                 axis=AX.X, op=ALU.max),
                  reads=[rg3], writes=[rgc])
            kb.op("v", lambda e: e.tensor_reduce(RW[:, 57:58], GP[:, 44:47],
                                                 axis=AX.X, op=ALU.min),
                  reads=[rg3], writes=[rgc])
            kb.op("v", lambda e: e.reduce_sum(RW[:, 58:59], GP[:, 44:47], axis=AX.X),
                  reads=[rg3], writes=[rgc])
            kb.op("v", lambda e: e.tensor_tensor(RW[:, 59:60], RW[:, 58:59],
                                                 RW[:, 56:57], ALU.subtract),
                  reads=[rgc], writes=[rgc])
            kb.op("v", lambda e: e.tensor_tensor(RW[:, 59:60], RW[:, 59:60],
                                                 RW[:, 57:58], ALU.subtract),
                  reads=[rgc], writes=[rgc])
            kb.op("v", lambda e: e.tensor_tensor(RW[:, 60:61], RW[:, 56:57],
                                                 RW[:, 59:60], ALU.add),
                  reads=[rgc], writes=[rgc])
            kb.op("v", lambda e: e.tensor_scalar_add(RW[:, 60:61], RW[:, 60:61],
                                                     1e-8), reads=[rgc], writes=[rgc])
            kb.op("v", lambda e: e.reciprocal(RW[:, 61:62], RW[:, 60:61]),
                  reads=[rgc], writes=[rgc])
            kb.op("v", lambda e: e.tensor_tensor(RW[:, 62:63], RW[:, 56:57],
                                                 RW[:, 61:62], ALU.mult),
                  reads=[rgc], writes=[rgc])
            kb.op("v", lambda e: e.tensor_tensor(RW[:, 63:64], RW[:, 59:60],
                                                 RW[:, 61:62], ALU.mult),
                  reads=[rgc], writes=[rgc])
            kb.op("t", lambda e: e.transpose(pC[0:1, 0:128], RW[:, 62:63], identf),
                  reads=[rgc, R("const")], writes=[R("pCx")])
            kb.op("v", lambda e, tt=tt: e.tensor_copy(
                ROW6[0:1, tt * 128:(tt + 1) * 128], pC[0:1, 0:128]),
                reads=[R("pCx")], writes=[rW12])
            kb.op("t", lambda e: e.transpose(pC[0:1, 0:128], RW[:, 63:64], identf),
                  reads=[rgc], writes=[R("pCx")])
            kb.op("v", lambda e, tt=tt: e.tensor_copy(
                ROW6[0:1, 1024 + tt * 128:1024 + (tt + 1) * 128], pC[0:1, 0:128]),
                reads=[R("pCx")], writes=[rW12])
        rjs = R("jsel")
        rfl = R("flg6")
        kb.op("v", lambda e: e.memset(SML[0:1, 48:49], 0.0), writes=[rsm])
        kb.op("v", lambda e: e.memset(SML[0:1, 49:50], 1.0), writes=[rsm])
        kb.op("v", lambda e: e.memset(SML[0:1, 50:51], 2.0), writes=[rsm])
        kb.op("v", lambda e: e.tensor_tensor(SML[0:1, 52:55], SML[0:1, 32:35],
                                             SML[0:1, 48:51], ALU.mult),
              reads=[rsm], writes=[rsm])
        kb.op("v", lambda e: e.reduce_sum(SML[0:1, 56:57], SML[0:1, 52:55],
                                          axis=AX.X), reads=[rsm], writes=[rsm])
        kb.op("v", lambda e: e.tensor_tensor(SML[0:1, 52:55], SML[0:1, 44:47],
                                             SML[0:1, 48:51], ALU.mult),
              reads=[rsm], writes=[rsm])
        kb.op("v", lambda e: e.reduce_sum(SML[0:1, 57:58], SML[0:1, 52:55],
                                          axis=AX.X), reads=[rsm], writes=[rsm])
        kb.op("v", lambda e: e.tensor_copy(jsel[0:1, 0:2], SML[0:1, 56:58]),
              reads=[rsm], writes=[rjs])
        kb.op("t", lambda e: e.matmul(pC[:, 0:3], onesf, SML[0:1, 32:35],
                                      start=True, stop=True),
              reads=[rsm, R("const")], writes=[R("pCx")])
        kb.op("v", lambda e: e.tensor_copy(RW[:, 41:44], pC[:, 0:3]),
              reads=[R("pCx")], writes=[rfl])
        kb.op("t", lambda e: e.matmul(pC[:, 0:3], onesf, SML[0:1, 44:47],
                                      start=True, stop=True),
              reads=[rsm], writes=[R("pCx")])
        kb.op("v", lambda e: e.tensor_copy(RW[:, 44:47], pC[:, 0:3]),
              reads=[R("pCx")], writes=[rfl])
        for p_ in range(2):
            for half in range(2):
                kb.op("t", lambda e, p_=p_, half=half: e.matmul(
                    pC, onesf,
                    ROW6[0:1, p_ * 1024 + half * 512:p_ * 1024 + (half + 1) * 512],
                    start=True, stop=True),
                    reads=[rW12, R("const")], writes=[R("pCx")])
                kb.op("v", lambda e, p_=p_, half=half: e.tensor_copy(
                    WBC[p_][:, half * 512:(half + 1) * 512], pC),
                    reads=[R("pCx")], writes=[rWBC])
        rATP_ = R("attnp")
        rATP_.did_read(sV, sV.n)
        rATP_.did_read(sT, sT.n)

    if upto >= 7:
        act = Bb
        rACT = rATP
        rMOE = R("moesc")
        ACTT = [Df[:, 5120 + i * 1024:5120 + (i + 1) * 1024] for i in range(3)]
        AMRG = Df[:, 4096:5120]
        jcell = [jsel[0:1, 0:1], jsel[0:1, 1:2]]
        rjs = R("jsel")
        rbs = R("bsel")
        rjsw = rjs.w_waits()
        pass_done = []
        for p_ in range(2):
            fcol = lambda k, p_=p_: RW[:, 41 + p_ * 3 + k:42 + p_ * 3 + k]
            rfl = R("flg6")

            def w1_evac(si, pm, rpm, p_=p_, fcol=fcol):
                ra = R("acttmp")
                rmrg = R("amrg")
                for k, fn_ in enumerate((AF.Tanh, AF.Gelu, AF.Silu)):
                    kb.op("s", lambda e, pm=pm, k=k, fn_=fn_, si=si, p_=p_:
                          e.activation(ACTT[k], pm, fn_,
                                       bias=bsel[:, p_ * DT + si:
                                                 p_ * DT + si + 1]),
                          reads=[rpm, rbs], writes=[ra])
                kb.op("v", lambda e, fcol=fcol: e.tensor_scalar_mul(
                    AMRG, ACTT[0], fcol(0)), reads=[ra, rfl], writes=[rmrg])
                kb.op("v", lambda e, fcol=fcol: e.scalar_tensor_tensor(
                    AMRG, ACTT[1], fcol(1), AMRG, ALU.mult, ALU.add),
                    reads=[ra, rfl, rmrg], writes=[rmrg])
                kb.op("v", lambda e, fcol=fcol, si=si: e.scalar_tensor_tensor(
                    act[:, si * 1024:(si + 1) * 1024], ACTT[2], fcol(2), AMRG,
                    ALU.mult, ALU.add),
                    reads=[ra, rfl, rmrg], writes=[rACT])

            def emit_bsel(p_=p_):
                dma3("bsel", bsel[:, p_ * DT:(p_ + 1) * DT],
                     [b1_d[:, e_ * DT:(e_ + 1) * DT] for e_ in range(3)],
                     reads=[rjs], writes=[rbs])
                dma3("bsel", bsel[:, (2 + p_) * DT:(3 + p_) * DT],
                     [b2_d[:, e_ * DT:(e_ + 1) * DT] for e_ in range(3)],
                     reads=[rjs], writes=[rbs])

            def emit_w1(p_=p_, w1_evac=w1_evac):
                stream_mm_sw([w1_d[e_] for e_ in range(3)], DT, A, rHF, w1_evac,
                             W7,
                             first_extra=(R("X2T0").w_waits() +
                                          R("SQ5").w_waits())
                             if p_ == 0 else ())

            def w2_evac(si, pm, rpm, p_=p_):
                for tb in range(2):
                    stg = STG[(si * 2 + tb) % 4]
                    rstg = R(f"STG{(si * 2 + tb) % 4}")
                    kb.op("v", lambda e, pm=pm, tb=tb, stg=stg, si=si, p_=p_: (
                        e.scalar_tensor_tensor(
                            stg, pm[:, tb * 512:(tb + 1) * 512],
                            bsel[:, (2 + p_) * DT + si:(2 + p_) * DT + si + 1],
                            WBC[p_][:, tb * 512:(tb + 1) * 512],
                            ALU.add, ALU.mult)),
                        reads=[rpm, rbs, rWBC], writes=[rstg])
                    extra = pass_done[p_ - 1] if p_ > 0 else []
                    kw = {} if p_ == 0 else {"accum_op": ALU.add}
                    kb.dma("g", f"moe{(si * 2 + tb) % 4}",
                           moesc[si * 128:(si + 1) * 128, tb * 512:(tb + 1) * 512],
                           stg, reads=[rstg], writes=[rMOE], extra=extra, **kw)

            def emit_w2(w2_evac=w2_evac):
                stream_mm_sw([w2_d[e_] for e_ in range(3)], DT, act, rACT,
                             w2_evac, W7)

            def emit_pass(p_=p_, emit_bsel=emit_bsel, emit_w1=emit_w1,
                          emit_w2=emit_w2):
                emit_bsel()
                emit_w1()
                emit_w2()

            g_switch_region(jcell[p_], rjsw, emit_pass)
            pass_done.append([(kb.chan(f"moe{i}"), kb.chan(f"moe{i}").n)
                              for i in range(4)])

    if upto >= 8:
        rMOE = R("moesc")
        rOUT = R("outd")
        for tt in range(TT):
            rmo = R("MO8")
            mw = [(kb.chan(f"moe{i}"), kb.chan(f"moe{i}").n) for i in range(4)] \
                if tt == 0 else []
            kb.dma("y", "mo8", MO8.rearrange("p (ct t) -> p ct t", t=128),
                   moesc.rearrange("(ct p) t -> p ct t", p=128)
                   [:, :, tt * 128:(tt + 1) * 128],
                   reads=[rMOE], writes=[rmo], extra=mw)
            rmt = R("MT8")
            for g4 in range(8):
                bank, rb = (pD, R("pD")) if g4 % 2 == 0 else (pC, R("pCx"))
                for k in range(4):
                    last = k == 3
                    kb.op("t", lambda e, b=bank, g4=g4, k=k: e.transpose(
                        b[:, k * 128:(k + 1) * 128],
                        MO8[:, (g4 * 4 + k) * 128:(g4 * 4 + k + 1) * 128], identf),
                        reads=[rmo, R("const")], writes=[rb], inc=last, record=last)
                kb.op("v", lambda e, b=bank, g4=g4: e.tensor_copy(
                    MT8[:, g4 * 512:(g4 + 1) * 512], b[:]),
                    reads=[rb], writes=[rmt])
            negm, rstd, r_c = ln_stats(rmt, MT8, SQ8, R("SQ8"), 32)
            rn8 = R("NRM8")
            kb.op("v", lambda e, nm=negm, rs=rstd: e.tensor_scalar(
                NRM8, MT8, nm, rs, ALU.add, ALU.mult),
                reads=[rmt, r_c], writes=[rn8])
            rxr = R(f"XR8{tt % 2}")
            kb.dma("y", f"xr{tt % 2}", XR8[tt % 2], x2sc[tt * 128:(tt + 1) * 128, :],
                   reads=[rX2], writes=[rxr],
                   extra=(R("wb0").w_waits() + R("wb1").w_waits() +
                          R("wb2").w_waits() + R("wb3").w_waits())
                   if tt < 2 else ())
            kb.op("v", lambda e, tt=tt: e.tensor_tensor(
                MO8, XR8[tt % 2], NRM8, ALU.add),
                reads=[rxr, rn8], writes=[rmo])
            kb.dma("y", "out", out_d[tt * 128:(tt + 1) * 128, :], MO8,
                   reads=[rmo], writes=[rOUT])
    else:
        rz = R("zero")
        kb.op("v", lambda e: e.memset(XT[0], 0.0), writes=[rz],
              extra=([(sT, sT.n), (sV, sV.n), (sA, sA.n), (sg, sg.n)] +
                     [(s, s.n) for s in kb.dsem.values() if s.n]))
        for tt in range(TT):
            kb.dma("y", "out", out_d[tt * 128:(tt + 1) * 128, :], XT[0], reads=[R("zero")])

    if tap is not None:
        arena = {"A": A, "Bb": Bb, "C": C, "Dd": Dd}[tap]
        tap_d = nc.dram_tensor("tap", [128, arena.shape[1]], BF16,
                               kind="ExternalOutput").ap()
        allw = [(s, s.n) for s in (sT, sV, sA, sg)]
        allw += [(s, s.n) for s in kb.dsem.values() if s.n]
        kb.dma("y", "out", tap_d, arena, extra=allw)
    def f_final(e):
        for s in kb.dsem.values():
            if s.n:
                e.wait_ge(s.h, s.n)
    kb.raw("y", f_final)
    kb.emit(blk)
    ctx.close()
    return nc



def _host_constants(half):
    T0 = half * TT
    pos = np.arange(half * T, (half + 1) * T, dtype=np.float64)
    inv = 10000.0 ** (-np.arange(64, dtype=np.float64) / 64.0)
    ang = inv[:, None] * pos[None, :]
    cosq = (np.cos(ang) * SM).astype(np.float32)
    sinq = (np.sin(ang) * SM).astype(np.float32)
    cosk = np.cos(ang).astype(np.float32)
    sink = np.sin(ang).astype(np.float32)
    pcol = np.arange(128)
    mo1 = np.zeros((128, TT, 1024), dtype=np.float32)
    mo1d = np.zeros((128, TT, 128), dtype=np.float32)
    for t in range(TT):
        ig = (T0 + t) * 128 + pcol
        jg = ((T0 + t - 8) * 128 + np.arange(1152))[None, :]
        ok = (jg >= 0) & (jg <= ig[:, None]) & (jg > ig[:, None] - WIN)
        m = np.where(ok, 0.0, NEG)
        mo1[:, t, :] = m[:, 0:1024]
        mo1d[:, t, :] = m[:, 1024:1152]
    return dict(
        cosq=cosq, sinq=sinq, cosk=cosk, sink=sink,
        mo1=np.ascontiguousarray(mo1.reshape(128, TT * 1024)).astype(BFNP),
        mo1d=np.ascontiguousarray(mo1d.reshape(128, TT * 128)).astype(BFNP),
        identf=np.eye(128, dtype=np.float32),
        onesb=np.ones((1, 128), dtype=BFNP),
        onesf=np.ones((1, 128), dtype=np.float32),
    )


def _prepare_inputs(inputs, nonce):
    x = np.ascontiguousarray(np.asarray(inputs["L0_attn_x"], dtype=np.float32))
    wq = np.ascontiguousarray(inputs["L0_attn_wq"])
    wk = np.ascontiguousarray(inputs["L0_attn_wk"])
    wv = np.ascontiguousarray(inputs["L0_attn_wv"])
    wo = np.ascontiguousarray(inputs["L0_attn_wo"])
    gw = np.asarray(inputs["L0_ffn_gate_w"])
    gb = np.asarray(inputs["L0_ffn_gate_b"])
    w1 = np.ascontiguousarray(inputs["L0_ffn_w1"])
    b1 = np.asarray(inputs["L0_ffn_b1"])
    w2 = np.ascontiguousarray(inputs["L0_ffn_w2"])
    b2 = np.asarray(inputs["L0_ffn_b2"])
    b1a = np.ascontiguousarray(
        b1.reshape(3, DT, 128).transpose(2, 0, 1).reshape(128, 3 * DT))
    b2a = np.ascontiguousarray(
        b2.reshape(3, DT, 128).transpose(2, 0, 1).reshape(128, 3 * DT))
    gwa = np.ascontiguousarray(
        gw[:D].reshape(DT, 128, 3).transpose(1, 0, 2).reshape(128, DT * 3))
    gwl = np.ascontiguousarray(gw[D:D + 1, :]).astype(np.float32)
    gba = np.ascontiguousarray(gb.reshape(1, 3)).astype(np.float32)
    in_maps = []
    for c in range(8):
        b_, h_ = c // 2, c % 2
        cst = _host_constants(h_)
        if h_ == 0:
            sel = np.array([[1, 0, 0, 1, 1, 1, 1, 0]], dtype=np.float32)
        else:
            sel = np.array([[0, 1, 1, 1, 0, 1, 0, 1]], dtype=np.float32)
        in_maps.append(dict(
            x_own=np.ascontiguousarray(x[b_, h_ * T:(h_ + 1) * T, :]),
            wq=wq, wk=wk, wv=wv, wo=wo, w1=w1, w2=w2,
            b1a=b1a, b2a=b2a, gwa=gwa, gwl=gwl, gb=gba,
            nonce=nonce, sel=sel, **cst))
    return in_maps


def kernel(**inputs):
    nonce = np.random.default_rng().integers(1, 2 ** 31 - 1, size=(1, 8),
                                             dtype=np.int32)
    in_maps = _prepare_inputs(inputs, nonce)
    if "full" not in _CACHE:
        _CACHE["full"] = build(upto=99)
    nc = _CACHE["full"]
    res = run_bass_kernel_spmd(nc, in_maps, core_ids=list(range(8)))
    out = np.empty((B, S, D), dtype=np.float32)
    for c in range(8):
        b_, h_ = c // 2, c % 2
        out[b_, h_ * T:(h_ + 1) * T, :] = res.results[c]["out"]
    return out


# version 12
# speedup vs baseline: 1.2366x; 1.2366x over previous
import numpy as np
import ml_dtypes
import concourse.bass as bass
import concourse.mybir as mybir
from concourse.bass_utils import run_bass_kernel_spmd

F32 = mybir.dt.float32
BF16 = mybir.dt.bfloat16
I32 = mybir.dt.int32
AF = mybir.ActivationFunctionType
ALU = mybir.AluOpType
AX = mybir.AxisListType
BFNP = ml_dtypes.bfloat16

B, S, D = 4, 2048, 4096
T = 1024
HD = 128
NQ, NKV = 32, 8
WIN = 1024
SM = float(HD) ** -0.5
NEG = -1.0e30
EPS = 1e-5
VIS = int(S * 0.3)
NSLOT = 16
DT = D // 128
TT = T // 128

_CACHE = {}


class _Sem:
    def __init__(self, name):
        self.name = name
        self.n = 0
        self.h = None


class Res:
    def __init__(self):
        self.w = {}
        self.rs = {}

    def r_waits(self):
        return list(self.w.items())

    def w_waits(self):
        d = dict(self.w)
        for s, v in self.rs.items():
            d[s] = max(d.get(s, 0), v)
        return list(d.items())

    def did_read(self, sem, val):
        self.rs[sem] = max(self.rs.get(sem, 0), val)

    def did_write(self, sem, val):
        self.w = {sem: val}
        self.rs = {}


class KB:
    def __init__(self, nc, ctx):
        self.nc = nc
        self.q = {k: [] for k in "tvsgy"}
        self.sem = {}
        for name in ("sT", "sV", "sA", "sg", "sy"):
            s = _Sem(name)
            s.h = ctx.enter_context(nc.semaphore(name))
            self.sem[name] = s
        self.esem = {"t": self.sem["sT"], "v": self.sem["sV"], "s": self.sem["sA"],
                     "g": self.sem["sg"], "y": self.sem["sy"]}
        self.watermark = {k: {} for k in "tvsgy"}
        self.rx = {}
        self.dsem = {}
        self._ctx = ctx

    def R(self, name):
        if name not in self.rx:
            self.rx[name] = Res()
        return self.rx[name]

    def op(self, ek, fn, reads=(), writes=(), extra=(), dma=False,
           inc=True, record=True):
        sem = self.esem[ek]
        waits = {}
        for r in reads:
            for s, v in r.r_waits():
                waits[s] = max(waits.get(s, 0), v)
        for w in writes:
            for s, v in w.w_waits():
                waits[s] = max(waits.get(s, 0), v)
        for s, v in extra:
            waits[s] = max(waits.get(s, 0), v)
        wm = self.watermark[ek]
        wl = []
        for s, v in sorted(waits.items(), key=lambda kv: kv[0].name):
            if v <= 0:
                continue
            if wm.get(s.name, -1) < v:
                wl.append((s, v))
                wm[s.name] = v
        incs = []
        if inc:
            sem.n += 16 if dma else 1
            incs = [(sem, 16 if dma else 1)]
        self.q[ek].append((wl, fn, incs))
        if record:
            sv = (sem, sem.n)
            for r in reads:
                r.did_read(*sv)
            for w in writes:
                w.did_write(*sv)
        return (sem, sem.n)

    def chan(self, name):
        if name not in self.dsem:
            s = _Sem("c_" + name)
            s.h = self._ctx.enter_context(self.nc.semaphore("c_" + name))
            self.dsem[name] = s
        return self.dsem[name]

    def dma(self, ek, chan, dst, src, reads=(), writes=(), extra=(), **kw):
        sem = self.chan(chan)
        waits = {}
        for r in reads:
            for s, v in r.r_waits():
                waits[s] = max(waits.get(s, 0), v)
        for w in writes:
            for s, v in w.w_waits():
                waits[s] = max(waits.get(s, 0), v)
        for s, v in extra:
            waits[s] = max(waits.get(s, 0), v)
        wm = self.watermark[ek]
        wl = []
        for s, v in sorted(waits.items(), key=lambda kv: kv[0].name):
            if v <= 0:
                continue
            if wm.get(s.name, -1) < v:
                wl.append((s, v))
                wm[s.name] = v
        sem.n += 16
        self.q[ek].append((wl, (lambda e, d=dst, s=src: e.dma_start(out=d, in_=s, **kw)),
                           [(sem, 16)]))
        sv = (sem, sem.n)
        for r in reads:
            r.did_read(*sv)
        for w in writes:
            w.did_write(*sv)
        return sv

    def raw(self, ek, fn):
        self.q[ek].append(([], fn, []))

    def emit(self, blk):
        def runner(items):
            def f(eng):
                for waits, fn, incs in items:
                    for sem, val in waits:
                        eng.wait_ge(sem.h, val)
                    inst = fn(eng) if fn is not None else None
                    for sem, k in incs:
                        if inst is not None:
                            inst.then_inc(sem.h, k)
            return f
        blk.tensor(runner(self.q["t"]))
        blk.vector(runner(self.q["v"]))
        blk.scalar(runner(self.q["s"]))
        blk.gpsimd(runner(self.q["g"]))
        blk.sync(runner(self.q["y"]))


def build(upto=99, expose=(), sim_single=False, tap=None):
    nc = bass.Bass("TRN2", target_bir_lowering=False)

    def dram(name, shape, dt, kind="Internal", **kw):
        k = "ExternalOutput" if name in expose else kind
        return nc.dram_tensor(name, shape, dt, kind=k, **kw).ap()

    x_in = dram("x_own", [T, D], F32, "ExternalInput")
    wq_d = dram("wq", [D, D], F32, "ExternalInput")
    wk_d = dram("wk", [D, NKV * HD], F32, "ExternalInput")
    wv_d = dram("wv", [D, NKV * HD], F32, "ExternalInput")
    wo_d = dram("wo", [D, D], F32, "ExternalInput")
    w1_d = dram("w1", [3, D, D], F32, "ExternalInput")
    w2_d = dram("w2", [3, D, D], F32, "ExternalInput")
    b1_d = dram("b1a", [128, 3 * DT], F32, "ExternalInput")
    b2_d = dram("b2a", [128, 3 * DT], F32, "ExternalInput")
    gw_d = dram("gwa", [128, DT * 3], F32, "ExternalInput")
    gwl_d = dram("gwl", [1, 3], F32, "ExternalInput")
    gb_d = dram("gb", [1, 3], F32, "ExternalInput")
    cosq_d = dram("cosq", [64, T], F32, "ExternalInput")
    sinq_d = dram("sinq", [64, T], F32, "ExternalInput")
    cosk_d = dram("cosk", [64, T], F32, "ExternalInput")
    sink_d = dram("sink", [64, T], F32, "ExternalInput")
    mo1_d = dram("mo1", [128, TT * 1024], BF16, "ExternalInput")
    mo1d_d = dram("mo1d", [128, TT * 128], BF16, "ExternalInput")
    identf_d = dram("identf", [128, 128], F32, "ExternalInput")
    onesb_d = dram("onesb", [1, 128], BF16, "ExternalInput")
    onesf_d = dram("onesf", [1, 128], F32, "ExternalInput")
    nonce_d = dram("nonce", [1, 8], I32, "ExternalInput")
    sel_d = dram("sel", [1, 8], F32, "ExternalInput")
    out_d = dram("out", [T, D], F32, "ExternalOutput")

    qsc = dram("qsc", [NQ * HD, T], BF16)
    ohsc = dram("ohsc", [NQ * HD, T], BF16)
    x2sc = dram("x2sc", [T, D], F32)
    moesc = dram("moesc", [D, T], F32)
    kvsh = dram("kvsh", [2, 128, 16384], BF16, addr_space="Shared")
    stsh = dram("stsh", [2, 1, 16], F32, addr_space="Shared")
    flsh = dram("flsh", [2, 2, 1, 8], I32, addr_space="Shared")

    from contextlib import ExitStack
    ctx = ExitStack()
    sbt = lambda name, shape, dt: ctx.enter_context(nc.sbuf_tensor(name, shape, dt))
    pst = lambda name, shape, dt: ctx.enter_context(nc.psum_tensor(name, shape, dt))

    A = sbt("A", [128, 32768], BF16)[:]
    Bb = sbt("Bb", [128, 32768], BF16)[:]
    C = sbt("C", [128, 16384], BF16)[:]
    Dd = sbt("Dd", [128, 16384], BF16)[:]
    identf = sbt("identf_s", [128, 128], F32)[:]
    identb = sbt("identb_s", [128, 128], BF16)[:]
    onesb = sbt("onesb_s", [1, 128], BF16)[:]
    onesf = sbt("onesf_s", [1, 128], F32)[:]
    b1s = sbt("b1s", [128, 3 * DT], F32)[:]
    b2s = sbt("b2s", [128, 3 * DT], F32)[:]
    gws = sbt("gws", [128, DT * 3], BF16)[:]
    gwlb = sbt("gwlb", [1, 3], F32)[:]
    gbs = sbt("gbs", [1, 3], F32)[:]
    nons = sbt("nons", [1, 8], I32)[:]
    sels = sbt("sels", [1, 8], F32)[:]
    jsel = sbt("jselbuf", [1, 8], I32)[:]
    bsel = sbt("bsel", [128, 4 * DT], F32)[:]
    RW = sbt("RW", [128, 192], F32)[:]
    GL = sbt("GL", [128, TT * 3], F32)[:]
    GP = sbt("GP", [128, 64], F32)[:]
    PAY = sbt("PAY", [1, 16], F32)[:]
    PPAY = sbt("PPAY", [1, 16], F32)[:]
    FLG = sbt("FLG", [1, 8], I32)[:]
    SML = sbt("SML", [1, 64], F32)[:]
    CEPS = sbt("CEPS", [128, 1], F32)[:]

    Af = A.bitcast(F32)
    Bf = Bb.bitcast(F32)
    Cf = C.bitcast(F32)
    Df = Dd.bitcast(F32)

    W2 = [Bb[:, 4096 + i * 4096:4096 + (i + 1) * 4096] for i in range(2)]
    W5 = [Dd[:, 4096 + i * 4096:4096 + (i + 1) * 4096] for i in range(2)]
    W7 = [C[:, i * 4096:(i + 1) * 4096] for i in range(4)]
    ROW6 = Bf[0:1, 0:3072]
    XT = [Bf[:, i * 4096:(i + 1) * 4096] for i in range(2)]
    NRM = [Bf[:, 8192 + i * 4096:8192 + (i + 1) * 4096] for i in range(2)]
    SQ1 = Cf[:, 0:4096]
    cosq = Df[0:64, 0:1024]
    sinq = Df[0:64, 1024:2048]
    cosk = Df[0:64, 2048:3072]
    sink = Df[0:64, 3072:4096]
    RP = [Bf[0:64, i * 512:(i + 1) * 512] for i in range(4)]
    SC1 = [Bf[:, i * 1152:(i + 1) * 1152] for i in range(3)]
    SC1E = [Bf[:, 3456 + i * 1152:3456 + (i + 1) * 1152] for i in range(3)]
    QH = [Bb[:, 15360 + i * 1024:15360 + (i + 1) * 1024] for i in range(2)]
    OG = [Bb[:, 17408 + i * 128:17408 + (i + 1) * 128] for i in range(2)]
    PBF = [Bb[:, 17664 + i * 1152:17664 + (i + 1) * 1152] for i in range(3)]
    PTS = [Bb[:, 21120 + i * 1152:21120 + (i + 1) * 1152] for i in range(3)]
    MK1 = A[:, 0:8192]
    MK1D = A[:, 8192:9216]
    SG5 = [Df[:, i * 1024:(i + 1) * 1024] for i in range(2)]
    XT5 = Df[:, 0:4096]
    NRM5 = Df[:, 4096:8192]
    X2T = [Cf[:, 0:4096], Cf[:, 0:4096]]
    SQ5 = Cf[:, 4096:8192]
    STG = [Df[:, i * 512:(i + 1) * 512] for i in range(4)]
    WBC = [Df[:, 2048 + i * 1024:2048 + (i + 1) * 1024] for i in range(3)]
    MO8x = [Df[:, 0:4096], Af[:, 0:4096]]
    MT8x = [Df[:, 4096:8192], Af[:, 4096:8192]]
    XR8 = [Cf[:, i * 4096:(i + 1) * 4096] for i in range(2)]
    SQ8x = [Bf[:, 0:4096], Bf[:, 8192:12288]]
    NRM8x = [Bf[:, 4096:8192], Bf[:, 12288:16384]]

    pM = [pst("pM0", [128, 1024], F32)[:], pst("pM1", [128, 1024], F32)[:]]
    pD = pst("pD", [128, 512], F32)[:]
    pT = [pst("pT0", [128, 1024], BF16)[:], pst("pT1", [128, 1024], BF16)[:]]
    pC = pst("pC", [128, 512], F32)[:]
    pCb = pC.bitcast(BF16)

    kb = KB(nc, ctx)
    R = kb.R
    sg, sy, sT, sV, sA = (kb.sem[n] for n in ("sg", "sy", "sT", "sV", "sA"))
    sX1 = _Sem("sX1"); sX1.h = ctx.enter_context(nc.semaphore("sX1"))
    sX2 = _Sem("sX2"); sX2.h = ctx.enter_context(nc.semaphore("sX2"))

    blk = ctx.enter_context(nc.Block())

    def ln_stats(rsrc, xa, sq_scratch, rsq, base):
        c = lambda i: RW[:, base + i:base + i + 1]
        r_c = R(f"rw{base}")
        kb.op("v", lambda e: e.reduce_sum(c(0), xa, axis=AX.X),
              reads=[rsrc], writes=[r_c])
        kb.op("s", lambda e: e.activation(sq_scratch, xa, AF.Square,
                                          accum_out=c(1)),
              reads=[rsrc], writes=[rsq, r_c])
        kb.op("v", lambda e: e.tensor_scalar_mul(c(2), c(0), 1.0 / D),
              reads=[r_c], writes=[r_c])
        kb.op("v", lambda e: e.tensor_tensor(c(4), c(2), c(2), ALU.mult),
              reads=[r_c], writes=[r_c])
        kb.op("v", lambda e: e.scalar_tensor_tensor(
            c(5), c(1), 1.0 / D, c(4), ALU.mult, ALU.subtract),
            reads=[r_c], writes=[r_c])
        kb.op("s", lambda e: e.activation(c(6), c(5), AF.Ln, bias=CEPS),
              reads=[r_c, R("const")], writes=[r_c])
        kb.op("s", lambda e: e.activation(c(7), c(6), AF.Exp, scale=-0.5),
              reads=[r_c], writes=[r_c])
        kb.op("v", lambda e: e.tensor_scalar_mul(c(8), c(2), -1.0),
              reads=[r_c], writes=[r_c])
        return c(8), c(7), r_c

    def transpose_block(src_f32, ncols, dst_bf, dcol0, t_off, rsrc, rdst):
        for g4 in range(0, ncols, 4):
            bank, rb = (pD, R("pD")) if (g4 // 4) % 2 == 0 else (pC, R("pCx"))
            cnt = min(4, ncols - g4)
            for k in range(cnt):
                last = k == cnt - 1
                kb.op("t", lambda e, b=bank, k=k, c0=(g4 + k) * 128: e.transpose(
                    b[:, k * 128:(k + 1) * 128], src_f32[:, c0:c0 + 128], identf),
                    reads=[rsrc, R("const")], writes=[rb],
                    inc=last, record=last)
            def cp(e, b=bank, g4=g4, cnt=cnt):
                dst = dst_bf.rearrange("p (dt t) -> p dt t", t=1024)[
                    :, dcol0 + g4:dcol0 + g4 + cnt, t_off:t_off + 128]
                return e.tensor_copy(
                    dst, b[:].rearrange("p (a b) -> p a b", b=128)[:, 0:cnt, :])
            kb.op("v", cp, reads=[rb], writes=[rdst])

    def stream_mm(w_dram, nslices, rhs_bf, r_rhs, evac, wbufs, wpre,
                  first_extra=()):
        for si in range(nslices):
            wbuf = si % 2
            rWb = R(f"wb{wbuf}")
            kb.dma("g", f"wb{wbuf}", wbufs[wbuf].rearrange("p (a c) -> p a c", c=128),
                   w_dram.rearrange("(ct p) c -> p ct c", p=128)
                   [:, :, si * 128:(si + 1) * 128],
                   writes=[rWb], extra=first_extra if si < 2 else ())
            pm, rpm = pM[si % 2], R(f"pM{si % 2}")
            for ct in range(DT):
                first, last = ct == 0, ct == DT - 1
                for tb in range(2):
                    lw = last and tb == 1
                    kb.op("t", lambda e, wbuf=wbuf, ct=ct, tb=tb, pm=pm, st_=first, sp_=last: e.matmul(
                        pm[:, tb * 512:(tb + 1) * 512],
                        wbufs[wbuf][:, ct * 128:(ct + 1) * 128],
                        rhs_bf[:, ct * 1024 + tb * 512: ct * 1024 + (tb + 1) * 512],
                        start=st_, stop=sp_),
                        reads=[rWb, r_rhs] if (first or lw) else [],
                        writes=[rpm] if (first or lw) else [],
                        inc=lw, record=lw)
            evac(si, pm, rpm)

    def dma3(chan, dst, srcs, reads=(), writes=(), extra=(), **kw):
        sem = kb.chan(chan)
        waits = {}
        for r in reads:
            for s, v in r.r_waits():
                waits[s] = max(waits.get(s, 0), v)
        for w in writes:
            for s, v in w.w_waits():
                waits[s] = max(waits.get(s, 0), v)
        for s, v in extra:
            waits[s] = max(waits.get(s, 0), v)
        wm = kb.watermark["g"]
        wl = []
        for s, v in sorted(waits.items(), key=lambda kv: kv[0].name):
            if v <= 0:
                continue
            if wm.get(s.name, -1) < v:
                wl.append((s, v))
                wm[s.name] = v
        sem.n += 16

        def f2(e, e_, dst=dst, srcs=srcs, kw=kw):
            return e.dma_start(out=dst, in_=srcs[e_], **kw)
        f2._sw3 = True
        kb.q["g"].append((wl, f2, [(sem, 16)]))
        sv = (sem, sem.n)
        for r in reads:
            r.did_read(*sv)
        for w in writes:
            w.did_write(*sv)
        return sv

    _gsw = [0]

    def g_switch_region(cell, rjs_waits, emit_fn):
        wm = kb.watermark["g"]
        wl = []
        for s, v in sorted(rjs_waits, key=lambda kv: kv[0].name):
            if v > 0 and wm.get(s.name, -1) < v:
                wl.append((s, v))
                wm[s.name] = v
        mark = len(kb.q["g"])
        emit_fn()
        items = kb.q["g"][mark:]
        del kb.q["g"][mark:]

        _gsw[0] += 1
        nm = f"gsw{_gsw[0]}"

        def fn(e, items=items, cell=cell, nm=nm):
            def arm(e_):
                for waits, f2, incs in items:
                    for sem, val in waits:
                        e.wait_ge(sem.h, val)
                    inst = f2(e, e_) if getattr(f2, "_sw3", False) else f2(e)
                    for sem, k in incs:
                        if inst is not None:
                            inst.then_inc(sem.h, k)
            with e.register(nm) as jr:
                e.reg_load(jr, cell)
                with e.If_eq(jr, 0):
                    arm(0)
                with e.Else():
                    with e.If_eq(jr, 1):
                        arm(1)
                    with e.Else():
                        arm(2)
            return None

        kb.q["g"].append((wl, fn, []))

    def stream_mm_sw(w_dram3, nslices, rhs_bf, r_rhs, evac, wbufs,
                     first_extra=()):
        nb = len(wbufs)

        def wdma(si):
            wbuf = si % nb
            rWb = R(f"wb{wbuf}")
            dma3(f"wb{wbuf}",
                 wbufs[wbuf].rearrange("p (a c) -> p a c", c=128),
                 [w_dram3[e_].rearrange("(ct p) c -> p ct c", p=128)
                  [:, :, si * 128:(si + 1) * 128] for e_ in range(3)],
                 reads=[R("jsel")], writes=[rWb],
                 extra=first_extra if si < nb else ())

        wdma(0)
        wdma(1)
        for si in range(nslices):
            if si + 2 < nslices:
                wdma(si + 2)
            wbuf = si % nb
            rWb = R(f"wb{wbuf}")
            pm, rpm = pM[si % 2], R(f"pM{si % 2}")
            for ct in range(DT):
                first, last = ct == 0, ct == DT - 1
                for tb in range(2):
                    lw = last and tb == 1
                    kb.op("t", lambda e, wbuf=wbuf, ct=ct, tb=tb, pm=pm, st_=first, sp_=last: e.matmul(
                        pm[:, tb * 512:(tb + 1) * 512],
                        wbufs[wbuf][:, ct * 128:(ct + 1) * 128],
                        rhs_bf[:, ct * 1024 + tb * 512: ct * 1024 + (tb + 1) * 512],
                        start=st_, stop=sp_),
                        reads=[rWb, r_rhs] if (first or lw) else [],
                        writes=[rpm] if (first or lw) else [],
                        inc=lw, record=lw)
            evac(si, pm, rpm)

    kb.op("v", lambda e: e.memset(CEPS, EPS), writes=[R("const")])
    for dst, src in ((identf, identf_d), (onesb, onesb_d), (onesf, onesf_d),
                     (b1s, b1_d), (b2s, b2_d), (gwlb, gwl_d),
                     (gbs, gb_d), (nons, nonce_d), (sels, sel_d)):
        kb.dma("y", "cst", dst, src, writes=[R("const")])
    kb.dma("g", "cstg", gws, gw_d, writes=[R("const")])
    rTAB = R("tables")
    for dst, src in ((cosq, cosq_d), (sinq, sinq_d), (cosk, cosk_d), (sink, sink_d)):
        kb.dma("y", "tab", dst, src, writes=[rTAB])

    h1 = A
    rH1 = R("h1")
    for tt in range(TT):
        bi = tt % 2
        rxt = R(f"XT{bi}")
        kb.dma("y", f"xt{bi}", XT[bi], x_in[tt * 128:(tt + 1) * 128, :], writes=[rxt])
        negm, rstd, r_c = ln_stats(rxt, XT[bi], SQ1, R("SQ1"), 0)
        rnrm = R(f"NRM{bi}")
        kb.op("v", lambda e, bi=bi, nm=negm, rs=rstd: e.tensor_scalar(
            NRM[bi], XT[bi], nm, rs, ALU.add, ALU.mult),
            reads=[rxt, r_c], writes=[rnrm])
        for g8 in range(4):
            transpose_block(NRM[bi][:, g8 * 1024:(g8 + 1) * 1024], 8,
                            h1, g8 * 8, tt * 128, rnrm, rH1)

    kall, vall = C, Dd
    rKA, rVA = R("kall"), R("vall")
    rQsc = R("qsc")

    if upto >= 2:
        def rope_evac(pp, rpp, cosT, sinT, dlo, dhi, tb, rdst):
            cc = cosT[:, tb * 512:(tb + 1) * 512]
            ss = sinT[:, tb * 512:(tb + 1) * 512]
            rrp = R("RP")
            kb.op("v", lambda e, pp=pp, cc=cc: e.tensor_tensor(
                RP[0], pp[0:64, :], cc, ALU.mult), reads=[rpp, rTAB], writes=[rrp])
            kb.op("v", lambda e, pp=pp, ss=ss: e.tensor_tensor(
                RP[1], pp[64:128, :], ss, ALU.mult), reads=[rpp, rTAB], writes=[rrp])
            kb.op("v", lambda e, dlo=dlo: e.tensor_tensor(
                dlo, RP[0], RP[1], ALU.subtract), reads=[rrp], writes=[rdst])
            kb.op("v", lambda e, pp=pp, ss=ss: e.tensor_tensor(
                RP[2], pp[0:64, :], ss, ALU.mult), reads=[rpp, rTAB], writes=[rrp])
            kb.op("v", lambda e, pp=pp, cc=cc: e.tensor_tensor(
                RP[3], pp[64:128, :], cc, ALU.mult), reads=[rpp, rTAB], writes=[rrp])
            kb.op("v", lambda e, dhi=dhi: e.tensor_tensor(
                dhi, RP[2], RP[3], ALU.add), reads=[rrp, rpp], writes=[rdst])

        p1w = []
        for rn in ("XT0", "XT1", "NRM0", "NRM1"):
            p1w += R(rn).w_waits()
        qkfirst = 0
        for kind, nheads, wsrc in (("q", NQ, wq_d), ("k", NKV, wk_d)):
            for head in range(nheads):
                wbuf = head % 2
                rWb = R(f"wb{wbuf}")
                kb.dma("g", f"wb{wbuf}", W2[wbuf].rearrange("p (a c) -> p a c", c=128),
                       wsrc.rearrange("(ct p) c -> p ct c", p=128)
                       [:, :, head * 128:(head + 1) * 128], writes=[rWb],
                       extra=p1w if qkfirst < 2 else ())
                qkfirst += 1
                pp, rpp = pM[head % 2], R(f"pM{head % 2}")
                for ct in range(DT):
                    first, last = ct == 0, ct == DT - 1
                    for tb in range(2):
                        lw = last and tb == 1
                        kb.op("t", lambda e, wbuf=wbuf, ct=ct, tb=tb, pp=pp, st_=first, sp_=last: e.matmul(
                            pp[:, tb * 512:(tb + 1) * 512],
                            W2[wbuf][:, ct * 128:(ct + 1) * 128],
                            h1[:, ct * 1024 + tb * 512: ct * 1024 + (tb + 1) * 512],
                            start=st_, stop=sp_),
                            reads=[rWb, rH1] if (first or lw) else [],
                            writes=[rpp] if (first or lw) else [],
                            inc=lw, record=lw)
                for tb in range(2):
                    if kind == "q":
                        rqh = R(f"QH{head % 2}")
                        rope_evac(pp[:, tb * 512:(tb + 1) * 512], rpp, cosq, sinq,
                                  QH[head % 2][0:64, tb * 512:(tb + 1) * 512],
                                  QH[head % 2][64:128, tb * 512:(tb + 1) * 512],
                                  tb, rqh)
                    else:
                        ko = head * NSLOT * 128 + 8 * 128 + tb * 512
                        rope_evac(pp[:, tb * 512:(tb + 1) * 512], rpp, cosk, sink,
                                  kall[0:64, ko:ko + 512], kall[64:128, ko:ko + 512],
                                  tb, rKA)
                if kind == "q":
                    kb.dma("y", f"qsc{head % 2}", qsc[head * 128:(head + 1) * 128, :], QH[head % 2],
                           reads=[R(f"QH{head % 2}")], writes=[rQsc])
        for vs in range(8):
            wbuf = vs % 2
            rWb = R(f"wb{wbuf}")
            kb.dma("g", f"wb{wbuf}", W2[wbuf].rearrange("p (a c) -> p a c", c=128),
                   wv_d.rearrange("(ct p) c -> p ct c", p=128)
                   [:, :, vs * 128:(vs + 1) * 128], writes=[rWb])
            for jt in range(TT):
                pp, rpp = (pC, R("pCx")) if jt % 2 == 0 else (pD, R("pD"))
                for ct in range(DT):
                    first, last = ct == 0, ct == DT - 1
                    kb.op("t", lambda e, wbuf=wbuf, ct=ct, jt=jt, pp=pp, st_=first, sp_=last: e.matmul(
                        pp[:, 0:128],
                        h1[:, ct * 1024 + jt * 128: ct * 1024 + (jt + 1) * 128],
                        W2[wbuf][:, ct * 128:(ct + 1) * 128],
                        start=st_, stop=sp_),
                        reads=[rWb, rH1] if (first or last) else [],
                        writes=[rpp] if (first or last) else [],
                        inc=last, record=last)
                vo = (8 + jt) * 1024 + vs * 128
                kb.op("v", lambda e, pp=pp, vo=vo: e.tensor_copy(
                    vall[:, vo:vo + 128], pp[:, 0:128]),
                    reads=[rpp], writes=[rVA])

    if upto >= 3:
        exw = {}
        for r in (rKA, rVA, rTAB, R("QH0"), R("QH1")):
            for s, v in r.w_waits():
                exw[s] = max(exw.get(s, 0), v)
        exw_list = sorted(exw.items(), key=lambda kv: kv[0].name)

        def f_exchange(g):
            for s, v in exw_list:
                g.wait_ge(s.h, v)
            with g.register("fva1") as fva, g.register("fvb1") as fvb, \
                    g.register("tgt1") as tgt, g.register("wna1") as wna, \
                    g.register("wnb1") as wnb:
                for arm in nc.switch_core_id(engines=[g], n=128):
                    me = 0 if sim_single else int(arm) % 2
                    other = 0 if sim_single else 1 - me
                    g.dma_start(
                        out=kvsh[me].rearrange("p (h j) -> p h j", j=1024)[:, 0:8, :],
                        in_=kall.rearrange("p (h s) -> p h s", s=NSLOT * 128)
                        [:, :, 8 * 128:16 * 128]).then_inc(sX1.h, 16)
                    g.dma_start(out=kvsh[me, :, 8192:16384],
                                in_=vall[:, 8 * 1024:16 * 1024]).then_inc(sX1.h, 16)
                    g.wait_ge(sX1.h, 32)
                    g.dma_start(out=flsh[0, me], in_=nons).then_inc(sX1.h, 16)
                    g.wait_ge(sX1.h, 48)
                    g.reg_load(wna, nons[0:1, 0:1])
                    g.reg_load(wnb, nons[0:1, 1:2])
                    g.reg_mov(fva, 1)
                    g.reg_mov(tgt, 64)
                    with g.While(fva):
                        g.dma_start(out=FLG, in_=flsh[0, other]).then_inc(sX1.h, 16)
                        g.wait_ge(sX1.h, tgt)
                        g.reg_add(tgt, tgt, 16)
                        g.reg_load(fva, FLG[0:1, 0:1])
                        g.reg_load(fvb, FLG[0:1, 1:2])
                        g.reg_alu(fva, fva, wna, op=ALU.subtract)
                        g.reg_alu(fvb, fvb, wnb, op=ALU.subtract)
                        g.reg_alu(fva, fva, fvb, op=ALU.bitwise_or)
                    g.dma_start(
                        out=kall.rearrange("p (h s) -> p h s", s=NSLOT * 128)
                        [:, :, 0:8 * 128],
                        in_=kvsh[other].rearrange("p (h j) -> p h j", j=1024)[:, 0:8, :]
                    ).then_inc(kb.chan("kvex").h, 16)
                    g.dma_start(out=vall[:, 0:8 * 1024],
                                in_=kvsh[other, :, 8192:16384]).then_inc(kb.chan("kvex").h, 16)

        kvex = kb.chan("kvex")

        if sim_single:
            kb.dma("g", "kvex",
                   kall.rearrange("p (h s) -> p h s", s=NSLOT * 128)[:, :, 0:8 * 128],
                   kall.rearrange("p (h s) -> p h s", s=NSLOT * 128)[:, :, 8 * 128:16 * 128],
                   extra=exw_list)
            kb.dma("g", "kvex", vall[:, 0:8 * 1024], vall[:, 8 * 1024:16 * 1024],
                   extra=exw_list)
            rKA.did_write(kvex, kvex.n)
            rVA.did_write(kvex, kvex.n)
        else:
            kb.raw("g", f_exchange)
            kvex.n += 32
            rKA.did_write(kvex, kvex.n)
            rVA.did_write(kvex, kvex.n)
        kb.watermark

    rMK = rH1
    if upto >= 4:
        kb.dma("y", "msk", MK1, mo1_d, writes=[rMK])
        kb.dma("y", "msk", MK1D, mo1d_d, writes=[rMK])
        kb.op("v", lambda e: e.tensor_copy(identb, identf),
              reads=[R("const")], writes=[R("constb")])
        qw = [(kb.chan("qsc0"), kb.chan("qsc0").n), (kb.chan("qsc1"), kb.chan("qsc1").n)]
        kb.dma("y", "qh0", QH[0], qsc[0:128, :], reads=[rQsc], writes=[R("QH0")],
               extra=qw)
        rOHs = R("ohsc")

        def p4_scores(idx):
            hq, ta = idx // 8, idx % 8
            h = hq // 4
            qb = hq % 2
            rqh = R(f"QH{qb}")
            if ta == 0 and hq + 1 < NQ:
                kb.dma("y", f"qh{(hq + 1) % 2}", QH[(hq + 1) % 2],
                       qsc[(hq + 1) * 128:(hq + 2) * 128, :],
                       reads=[rQsc], writes=[R(f"QH{(hq + 1) % 2}")], extra=qw)
            pm, rpm = pM[idx % 2], R(f"pM{idx % 2}")
            k0 = h * NSLOT * 128 + ta * 128
            for pc in range(2):
                kb.op("t", lambda e, pm=pm, qb=qb, ta=ta, k0=k0, pc=pc:
                      e.matmul(pm[:, pc * 512:(pc + 1) * 512],
                               QH[qb][:, ta * 128:(ta + 1) * 128],
                               kall[:, k0 + pc * 512:k0 + (pc + 1) * 512],
                               start=True, stop=True),
                      reads=[rqh, rKA], writes=[rpm],
                      inc=(pc == 1), record=(pc == 1))
            dreg = pD[:, (idx % 4) * 128:(idx % 4 + 1) * 128]
            rdr = R(f"pD{idx % 4}")
            kb.op("t", lambda e, dreg=dreg, qb=qb, ta=ta, k0=k0: e.matmul(
                dreg, QH[qb][:, ta * 128:(ta + 1) * 128],
                kall[:, k0 + 1024:k0 + 1152], start=True, stop=True),
                reads=[rqh, rKA], writes=[rdr])

        def p4_A(idx):
            hq, ta = idx // 8, idx % 8
            bi = idx % 3
            pm, rpm = pM[idx % 2], R(f"pM{idx % 2}")
            dreg = pD[:, (idx % 4) * 128:(idx % 4 + 1) * 128]
            rdr = R(f"pD{idx % 4}")
            rsc = R(f"SC1_{bi}")
            kb.op("v", lambda e, bi=bi, pm=pm, ta=ta: e.scalar_tensor_tensor(
                SC1[bi][:, 0:1024], pm, 1.0, MK1[:, ta * 1024:(ta + 1) * 1024],
                ALU.mult, ALU.add), reads=[rpm, rMK], writes=[rsc])
            kb.op("v", lambda e, bi=bi, dreg=dreg, ta=ta: e.scalar_tensor_tensor(
                SC1[bi][:, 1024:1152], dreg, 1.0,
                MK1D[:, ta * 128:(ta + 1) * 128], ALU.mult, ALU.add),
                reads=[rdr, rMK], writes=[rsc])
            cc = lambda i, idx=idx: RW[:, 16 + (idx % 4) * 4 + i:
                                       16 + (idx % 4) * 4 + i + 1]
            rcol = R(f"acol{idx % 4}")
            kb.op("v", lambda e, bi=bi, cc=cc: e.reduce_max(
                cc(0), SC1[bi], axis=AX.X, negate=True),
                reads=[rsc], writes=[rcol])
            rse = R(f"SC1E_{bi}")
            kb.op("s", lambda e, bi=bi, cc=cc: e.activation(
                SC1E[bi], SC1[bi], AF.Exp, bias=cc(0), accum_out=cc(1)),
                reads=[rsc, rcol], writes=[rse, rcol])

        def p4_B(idx):
            bi = idx % 3
            cc = lambda i, idx=idx: RW[:, 16 + (idx % 4) * 4 + i:
                                       16 + (idx % 4) * 4 + i + 1]
            rcol = R(f"acol{idx % 4}")
            rse = R(f"SC1E_{bi}")
            kb.op("v", lambda e, cc=cc: e.reciprocal(cc(2), cc(1)),
                  reads=[rcol], writes=[rcol])
            rpb = R(f"PBF_{bi}")
            kb.op("s", lambda e, bi=bi, cc=cc: e.mul(
                PBF[bi], SC1E[bi], cc(2)),
                reads=[rse, rcol], writes=[rpb])
            rpT = R(f"pT{idx % 2}")
            for sl in range(8):
                kb.op("t", lambda e, bi=bi, sl=sl, b2=idx % 2: e.transpose(
                    pT[b2][:, sl * 128:(sl + 1) * 128],
                    PBF[bi][:, sl * 128:(sl + 1) * 128], identb),
                    reads=[rpb, R("constb")], writes=[rpT],
                    inc=(sl == 7), record=(sl == 7))
            rpd = R(f"pCd{idx % 2}")
            kb.op("t", lambda e, bi=bi, b2=idx % 2: e.transpose(
                pCb[:, 512 + b2 * 128:512 + (b2 + 1) * 128],
                PBF[bi][:, 1024:1152], identb),
                reads=[rpb], writes=[rpd])

        def p4_C(idx):
            hq, ta = idx // 8, idx % 8
            h = hq // 4
            bi = idx % 3
            b2 = idx % 2
            rpT = R(f"pT{b2}")
            rpd = R(f"pCd{b2}")
            rpts = R(f"PTS_{bi}")
            kb.op("s", lambda e, bi=bi, b2=b2: e.copy(PTS[bi][:, 0:1024],
                                                      pT[b2]),
                  reads=[rpT], writes=[rpts])
            kb.op("v", lambda e, bi=bi, b2=b2: e.tensor_copy(
                PTS[bi][:, 1024:1152],
                pCb[:, 512 + b2 * 128:512 + (b2 + 1) * 128]),
                reads=[rpd], writes=[rpts])
            pcr = pC[:, b2 * 128:(b2 + 1) * 128]
            rpc = R(f"pC{b2}")
            for s_ in range(9):
                kb.op("t", lambda e, s_=s_, ta=ta, h=h, bi=bi, pcr=pcr: e.matmul(
                    pcr, vall[:, (ta + s_) * 1024 + h * 128:
                              (ta + s_) * 1024 + (h + 1) * 128],
                    PTS[bi][:, s_ * 128:(s_ + 1) * 128],
                    start=(s_ == 0), stop=(s_ == 8)),
                    reads=[rpts, rVA] if s_ in (0, 8) else [],
                    writes=[rpc] if s_ in (0, 8) else [],
                    inc=(s_ == 8), record=(s_ == 8))
            rog = R(f"OG{idx % 2}")
            kb.op("s", lambda e, pcr=pcr, og=OG[idx % 2]: e.copy(og, pcr),
                  reads=[rpc], writes=[rog])
            kb.dma("y", f"ohs{idx % 2}",
                   ohsc[hq * 128:(hq + 1) * 128, ta * 128:(ta + 1) * 128],
                   OG[idx % 2], reads=[rog], writes=[rOHs])

        NTIL = NQ * 8
        p4_scores(0)
        for idx in range(NTIL):
            if idx + 1 < NTIL:
                p4_scores(idx + 1)
            p4_A(idx)
            if idx >= 2:
                p4_C(idx - 2)
            if idx >= 1:
                p4_B(idx - 1)
        p4_C(NTIL - 2)
        p4_B(NTIL - 1)
        p4_C(NTIL - 1)

    rATP = R("attnp")
    rHF = rMK
    rX2 = R("x2sc")
    if upto >= 5:
        ohw = [(kb.chan("ohs0"), kb.chan("ohs0").n),
               (kb.chan("ohs1"), kb.chan("ohs1").n)]
        kb.dma("y", "ohr", A.rearrange("p (ht t) -> p ht t", t=1024),
               ohsc.rearrange("(ht p) t -> p ht t", p=128),
               reads=[rOHs], writes=[rMK], extra=ohw)
        attnp = Bb

        def wo_evac(si, pm, rpm):
            sg5, rsg = SG5[si % 2], R(f"SG5{si % 2}")
            kb.op("v", lambda e, pm=pm, sg5=sg5: e.tensor_copy(sg5, pm),
                  reads=[rpm], writes=[rsg])
            for g4 in range(2):
                bank, rb = (pD, R("pD")) if g4 == 0 else (pC, R("pCx"))
                for k in range(4):
                    last = k == 3
                    kb.op("t", lambda e, b=bank, sg5=sg5, g4=g4, k=k: e.transpose(
                        b[:, k * 128:(k + 1) * 128],
                        sg5[:, (g4 * 4 + k) * 128:(g4 * 4 + k + 1) * 128], identf),
                        reads=[rsg, R("const")], writes=[rb], inc=last, record=last)
                def cp5(e, b=bank, g4=g4, si=si):
                    dst = attnp.rearrange("p (tt d) -> p tt d", d=D)[
                        :, g4 * 4:g4 * 4 + 4, si * 128:(si + 1) * 128]
                    return e.tensor_copy(
                        dst, b[:].rearrange("p (a b) -> p a b", b=128))
                kb.op("v", cp5, reads=[rb], writes=[rATP])

        stream_mm(wo_d, DT, A, rMK, wo_evac, W5, "W5",
                  first_extra=rVA.w_waits())

        hffn = A
        for tt in range(TT):
            atile = attnp[:, tt * D:(tt + 1) * D]
            negm, rstd, r_c = ln_stats(rATP, atile, SQ5, R("SQ5"), 0)
            rn5 = R("NRM5")
            kb.op("v", lambda e, atile=atile, nm=negm, rs=rstd: e.tensor_scalar(
                NRM5, atile, nm, rs, ALU.add, ALU.mult),
                reads=[rATP, r_c], writes=[rn5])
            rxt5 = R("XT5")
            kb.dma("y", "xt5", XT5, x_in[tt * 128:(tt + 1) * 128, :], writes=[rxt5],
                   extra=(R("SG50").w_waits() + R("SG51").w_waits()))
            x2t, rx2t = X2T[0], R("X2T0")
            kb.op("v", lambda e, x2t=x2t: e.tensor_tensor(x2t, XT5, NRM5, ALU.add),
                  reads=[rxt5, rn5], writes=[rx2t])
            kb.dma("y", "x2w", x2sc[tt * 128:(tt + 1) * 128, :], x2t,
                   reads=[rx2t], writes=[rX2])
            negm2, rstd2, r_c2 = ln_stats(rx2t, x2t, SQ5, R("SQ5"), 16)
            kb.op("v", lambda e, x2t=x2t, nm=negm2, rs=rstd2: e.tensor_scalar(
                NRM5, x2t, nm, rs, ALU.add, ALU.mult),
                reads=[rx2t, r_c2], writes=[rn5])
            for g8 in range(4):
                transpose_block(NRM5[:, g8 * 1024:(g8 + 1) * 1024], 8,
                                hffn, g8 * 8, tt * 128, rn5, rHF)
            rGL = R("glraw")
            pb, rpb = (pC, R("pCx")) if tt % 2 == 0 else (pD, R("pD"))
            for ct in range(DT):
                first, last = ct == 0, ct == DT - 1
                kb.op("t", lambda e, ct=ct, tt=tt, pb=pb, st_=first, sp_=last: e.matmul(
                    pb[:, 0:3],
                    hffn[:, ct * 1024 + tt * 128: ct * 1024 + (tt + 1) * 128],
                    gws[:, ct * 3:(ct + 1) * 3], start=st_, stop=sp_),
                    reads=[rHF, R("const")] if (first or last) else [],
                    writes=[rpb] if (first or last) else [],
                    inc=last, record=last)
            kb.op("v", lambda e, pb=pb, tt=tt: e.tensor_copy(
                GL[:, tt * 3:(tt + 1) * 3], pb[:, 0:3]),
                reads=[rpb], writes=[rGL])

    rWBC = R("wbc")
    if upto >= 6:
        hffn = A
        rGPa, rGPb = R("gpa"), R("gpb")
        rsA_, rsB_ = R("sumA"), R("sumB")
        for ct in range(DT):
            kb.op("v", lambda e, ct=ct: e.tensor_reduce(
                GP[:, ct:ct + 1], hffn[:, ct * 1024:ct * 1024 + VIS],
                axis=AX.X, op=ALU.add, apply_absolute_value=True),
                reads=[rHF], writes=[rGPa])
        kb.op("v", lambda e: e.reduce_sum(RW[:, 48:49], GP[:, 0:DT], axis=AX.X),
              reads=[rGPa], writes=[rsA_])
        for ct in range(DT):
            kb.op("v", lambda e, ct=ct: e.tensor_reduce(
                GP[:, ct:ct + 1], hffn[:, ct * 1024 + VIS:(ct + 1) * 1024],
                axis=AX.X, op=ALU.add, apply_absolute_value=True),
                reads=[rHF], writes=[rGPb], extra=rsA_.w_waits())
        kb.op("v", lambda e: e.reduce_sum(RW[:, 49:50], GP[:, 0:DT], axis=AX.X),
              reads=[rGPb], writes=[rsB_])
        kb.op("t", lambda e: e.transpose(pD[0:1, 0:128], RW[:, 48:49], identf),
              reads=[rsA_, R("const")], writes=[R("pD")])
        kb.op("t", lambda e: e.transpose(pC[0:1, 0:128], RW[:, 49:50], identf),
              reads=[rsB_], writes=[R("pCx")])
        rpay = R("pay")
        kb.op("v", lambda e: e.memset(PAY, 0.0), writes=[rpay])
        kb.op("v", lambda e: e.reduce_sum(PAY[0:1, 0:1], pD[0:1, 0:128], axis=AX.X),
              reads=[R("pD")], writes=[rpay])
        kb.op("v", lambda e: e.reduce_sum(PAY[0:1, 1:2], pC[0:1, 0:128], axis=AX.X),
              reads=[R("pCx")], writes=[rpay])
        rGL = R("glraw")
        kb.op("v", lambda e: e.tensor_copy(PAY[0:1, 2:5], GL[0:1, 0:3]),
              reads=[rGL], writes=[rpay])

        payw = sorted(rpay.w_waits(), key=lambda kv: kv[0].name)

        def f_exch2(g):
            for s, v in payw:
                g.wait_ge(s.h, v)
            with g.register("fva2") as fva, g.register("fvb2") as fvb, \
                    g.register("tgt2") as tgt, g.register("wna2") as wna, \
                    g.register("wnb2") as wnb:
                for arm in nc.switch_core_id(engines=[g], n=128):
                    me = 0 if sim_single else int(arm) % 2
                    other = 0 if sim_single else 1 - me
                    g.dma_start(out=stsh[me], in_=PAY).then_inc(sX2.h, 16)
                    g.wait_ge(sX2.h, 16)
                    g.dma_start(out=flsh[1, me], in_=nons).then_inc(sX2.h, 16)
                    g.wait_ge(sX2.h, 32)
                    g.reg_load(wna, nons[0:1, 2:3])
                    g.reg_load(wnb, nons[0:1, 3:4])
                    g.reg_mov(fva, 1)
                    g.reg_mov(tgt, 48)
                    with g.While(fva):
                        g.dma_start(out=FLG, in_=flsh[1, other]).then_inc(sX2.h, 16)
                        g.wait_ge(sX2.h, tgt)
                        g.reg_add(tgt, tgt, 16)
                        g.reg_load(fva, FLG[0:1, 2:3])
                        g.reg_load(fvb, FLG[0:1, 3:4])
                        g.reg_alu(fva, fva, wna, op=ALU.subtract)
                        g.reg_alu(fvb, fvb, wnb, op=ALU.subtract)
                        g.reg_alu(fva, fva, fvb, op=ALU.bitwise_or)
                    g.dma_start(out=PPAY, in_=stsh[other]).then_inc(kb.chan("st2").h, 16)

        st2 = kb.chan("st2")
        if sim_single:
            rppy = R("ppay")
            kb.dma("g", "st2", PPAY, PAY, reads=[rpay], writes=[rppy], extra=payw)
        else:
            kb.raw("g", f_exch2)
            st2.n += 16
            rppy = R("ppay")
            rppy.did_write(st2, st2.n)
            rpay.did_read(st2, st2.n)

        sm_ = lambda i: SML[0:1, i:i + 1]
        rsm = R("sml")
        sap = lambda i: sels[0:1, i:i + 1]
        kb.op("v", lambda e: e.tensor_scalar_mul(sm_(0), PAY[0:1, 0:1], sap(0)),
              reads=[rpay, R("const")], writes=[rsm])
        kb.op("v", lambda e: e.scalar_tensor_tensor(
            sm_(0), PPAY[0:1, 0:1], sap(1), sm_(0), ALU.mult, ALU.add),
            reads=[rppy, rsm], writes=[rsm])
        kb.op("v", lambda e: e.tensor_scalar_mul(sm_(1), PAY[0:1, 0:1], sap(2)),
              reads=[rpay], writes=[rsm])
        kb.op("v", lambda e: e.scalar_tensor_tensor(
            sm_(1), PAY[0:1, 1:2], sap(3), sm_(1), ALU.mult, ALU.add),
            reads=[rpay, rsm], writes=[rsm])
        kb.op("v", lambda e: e.scalar_tensor_tensor(
            sm_(1), PPAY[0:1, 0:1], sap(4), sm_(1), ALU.mult, ALU.add),
            reads=[rppy, rsm], writes=[rsm])
        kb.op("v", lambda e: e.scalar_tensor_tensor(
            sm_(1), PPAY[0:1, 1:2], sap(5), sm_(1), ALU.mult, ALU.add),
            reads=[rppy, rsm], writes=[rsm])
        kb.op("v", lambda e: e.tensor_scalar(
            sm_(2), sm_(0), 1.0 / (VIS * D), 1e-8, ALU.mult, ALU.add),
            reads=[rsm], writes=[rsm])
        kb.op("v", lambda e: e.reciprocal(sm_(3), sm_(2)), reads=[rsm], writes=[rsm])
        kb.op("v", lambda e: e.tensor_scalar_mul(
            sm_(4), sm_(1), 1.0 / ((S - VIS) * D)), reads=[rsm], writes=[rsm])
        kb.op("v", lambda e: e.tensor_tensor(sm_(5), sm_(4), sm_(3), ALU.mult),
              reads=[rsm], writes=[rsm])
        kb.op("v", lambda e: e.tensor_scalar_mul(SML[0:1, 8:11], PAY[0:1, 2:5],
                                                 sap(6)), reads=[rpay], writes=[rsm])
        kb.op("v", lambda e: e.scalar_tensor_tensor(
            SML[0:1, 8:11], PPAY[0:1, 2:5], sap(7), SML[0:1, 8:11],
            ALU.mult, ALU.add), reads=[rppy, rsm], writes=[rsm])
        kb.op("v", lambda e: e.tensor_scalar_mul(SML[0:1, 12:15], gwlb, sm_(5)),
              reads=[R("const"), rsm], writes=[rsm])
        kb.op("v", lambda e: e.tensor_tensor(
            SML[0:1, 8:11], SML[0:1, 8:11], SML[0:1, 12:15], ALU.add),
            reads=[rsm], writes=[rsm])
        kb.op("v", lambda e: e.tensor_tensor(
            SML[0:1, 8:11], SML[0:1, 8:11], gbs, ALU.add),
            reads=[rsm, R("const")], writes=[rsm])
        kb.op("v", lambda e: e.tensor_reduce(sm_(16), SML[0:1, 8:11], axis=AX.X,
                                             op=ALU.max), reads=[rsm], writes=[rsm])
        kb.op("v", lambda e: e.tensor_scalar_mul(sm_(17), sm_(16), -1.0),
              reads=[rsm], writes=[rsm])
        kb.op("s", lambda e: e.activation(SML[0:1, 18:21], SML[0:1, 8:11], AF.Exp,
                                          bias=sm_(17), accum_out=sm_(21)),
              reads=[rsm], writes=[rsm])
        kb.op("v", lambda e: e.reciprocal(sm_(22), sm_(21)), reads=[rsm], writes=[rsm])
        kb.op("v", lambda e: e.tensor_scalar_mul(SML[0:1, 24:27], SML[0:1, 18:21],
                                                 sm_(22)), reads=[rsm], writes=[rsm])
        kb.op("v", lambda e: e.tensor_reduce(sm_(28), SML[0:1, 24:27], axis=AX.X,
                                             op=ALU.max), reads=[rsm], writes=[rsm])
        kb.op("v", lambda e: e.tensor_reduce(sm_(29), SML[0:1, 24:27], axis=AX.X,
                                             op=ALU.min), reads=[rsm], writes=[rsm])
        kb.op("v", lambda e: e.tensor_single_scalar(
            SML[0:1, 32:35], SML[0:1, 24:27], sm_(28), ALU.is_ge),
            reads=[rsm], writes=[rsm])
        kb.op("v", lambda e: e.tensor_single_scalar(
            SML[0:1, 36:39], SML[0:1, 24:27], sm_(29), ALU.is_le),
            reads=[rsm], writes=[rsm])
        kb.op("v", lambda e: e.tensor_tensor(
            SML[0:1, 40:43], SML[0:1, 32:35], SML[0:1, 36:39], ALU.add),
            reads=[rsm], writes=[rsm])
        kb.op("v", lambda e: e.tensor_scalar(
            SML[0:1, 44:47], SML[0:1, 40:43], -1.0, 1.0, ALU.mult, ALU.add),
            reads=[rsm], writes=[rsm])
        kb.op("v", lambda e: e.tensor_tensor(
            SML[0:1, 12:15], SML[0:1, 12:15], gbs, ALU.add),
            reads=[rsm, R("const")], writes=[rsm])
        kb.op("t", lambda e: e.matmul(pD[:, 0:3], onesf, SML[0:1, 12:15],
                                      start=True, stop=True),
              reads=[rsm, R("const")], writes=[R("pD")])
        rcorr = R("corrb")
        kb.op("v", lambda e: e.tensor_copy(GP[:, 32:35], pD[:, 0:3]),
              reads=[R("pD")], writes=[rcorr])
        rW12 = R("w12rows")
        rgv = R("gvec")
        GLv = GL.rearrange("p (t e) -> p t e", e=3)
        NW = lambda i: RW[:, 64 + i * TT:64 + (i + 1) * TT]
        for e_ in range(3):
            kb.op("v", lambda e, e_=e_: e.tensor_scalar_add(
                NW(e_), GLv[:, :, e_], GP[:, 32 + e_:33 + e_]),
                reads=[rGL, rcorr], writes=[rgv])
        kb.op("v", lambda e: e.tensor_tensor(NW(3), NW(0), NW(1), ALU.max),
              reads=[rgv], writes=[rgv])
        kb.op("v", lambda e: e.tensor_tensor(NW(3), NW(3), NW(2), ALU.max),
              reads=[rgv], writes=[rgv])
        for e_ in range(3):
            kb.op("v", lambda e, e_=e_: e.tensor_tensor(
                NW(e_), NW(e_), NW(3), ALU.subtract), reads=[rgv], writes=[rgv])
        for e_ in range(3):
            kb.op("s", lambda e, e_=e_: e.activation(NW(4 + e_), NW(e_), AF.Exp),
                  reads=[rgv], writes=[rgv])
        kb.op("v", lambda e: e.tensor_tensor(NW(7), NW(4), NW(5), ALU.add),
              reads=[rgv], writes=[rgv])
        kb.op("v", lambda e: e.tensor_tensor(NW(7), NW(7), NW(6), ALU.add),
              reads=[rgv], writes=[rgv])
        kb.op("v", lambda e: e.reciprocal(NW(8), NW(7)), reads=[rgv], writes=[rgv])
        kb.op("v", lambda e: e.tensor_tensor(NW(9), NW(4), NW(5), ALU.min),
              reads=[rgv], writes=[rgv])
        kb.op("v", lambda e: e.tensor_tensor(NW(9), NW(9), NW(6), ALU.min),
              reads=[rgv], writes=[rgv])
        kb.op("v", lambda e: e.tensor_tensor(NW(10), NW(7), NW(9), ALU.subtract),
              reads=[rgv], writes=[rgv])
        kb.op("v", lambda e: e.tensor_scalar_add(NW(10), NW(10), -1.0),
              reads=[rgv], writes=[rgv])
        kb.op("v", lambda e: e.tensor_tensor(NW(11), NW(10), NW(8), ALU.mult),
              reads=[rgv], writes=[rgv])
        kb.op("v", lambda e: e.tensor_tensor(NW(12), NW(8), NW(11), ALU.add),
              reads=[rgv], writes=[rgv])
        kb.op("v", lambda e: e.tensor_scalar_add(NW(12), NW(12), 1e-8),
              reads=[rgv], writes=[rgv])
        kb.op("v", lambda e: e.reciprocal(NW(13), NW(12)), reads=[rgv], writes=[rgv])
        kb.op("v", lambda e: e.tensor_tensor(NW(14), NW(8), NW(13), ALU.mult),
              reads=[rgv], writes=[rgv])
        kb.op("v", lambda e: e.tensor_tensor(NW(15), NW(11), NW(13), ALU.mult),
              reads=[rgv], writes=[rgv])
        for tt in range(TT):
            bank, rb = (pC, R("pCx")) if tt % 2 == 0 else (pD, R("pD"))
            kb.op("t", lambda e, tt=tt, bank=bank: e.transpose(
                bank[0:1, 0:128], NW(14)[:, tt:tt + 1], identf),
                reads=[rgv, R("const")], writes=[rb])
            kb.op("v", lambda e, tt=tt, bank=bank: e.tensor_copy(
                ROW6[0:1, tt * 128:(tt + 1) * 128], bank[0:1, 0:128]),
                reads=[rb], writes=[rW12])
            kb.op("t", lambda e, tt=tt, bank=bank: e.transpose(
                bank[0:1, 0:128], NW(15)[:, tt:tt + 1], identf),
                reads=[rgv], writes=[rb])
            kb.op("v", lambda e, tt=tt, bank=bank: e.tensor_copy(
                ROW6[0:1, 1024 + tt * 128:1024 + (tt + 1) * 128],
                bank[0:1, 0:128]),
                reads=[rb], writes=[rW12])
        rjs = R("jsel")
        rfl = R("flg6")
        kb.op("v", lambda e: e.memset(SML[0:1, 48:49], 0.0), writes=[rsm])
        kb.op("v", lambda e: e.memset(SML[0:1, 49:50], 1.0), writes=[rsm])
        kb.op("v", lambda e: e.memset(SML[0:1, 50:51], 2.0), writes=[rsm])
        kb.op("v", lambda e: e.tensor_tensor(SML[0:1, 52:55], SML[0:1, 32:35],
                                             SML[0:1, 48:51], ALU.mult),
              reads=[rsm], writes=[rsm])
        kb.op("v", lambda e: e.reduce_sum(SML[0:1, 56:57], SML[0:1, 52:55],
                                          axis=AX.X), reads=[rsm], writes=[rsm])
        kb.op("v", lambda e: e.tensor_tensor(SML[0:1, 52:55], SML[0:1, 44:47],
                                             SML[0:1, 48:51], ALU.mult),
              reads=[rsm], writes=[rsm])
        kb.op("v", lambda e: e.reduce_sum(SML[0:1, 57:58], SML[0:1, 52:55],
                                          axis=AX.X), reads=[rsm], writes=[rsm])
        kb.op("v", lambda e: e.tensor_copy(jsel[0:1, 0:2], SML[0:1, 56:58]),
              reads=[rsm], writes=[rjs])
        kb.op("t", lambda e: e.matmul(pC[:, 0:3], onesf, SML[0:1, 32:35],
                                      start=True, stop=True),
              reads=[rsm, R("const")], writes=[R("pCx")])
        kb.op("v", lambda e: e.tensor_copy(RW[:, 41:44], pC[:, 0:3]),
              reads=[R("pCx")], writes=[rfl])
        kb.op("t", lambda e: e.matmul(pC[:, 0:3], onesf, SML[0:1, 44:47],
                                      start=True, stop=True),
              reads=[rsm], writes=[R("pCx")])
        kb.op("v", lambda e: e.tensor_copy(RW[:, 44:47], pC[:, 0:3]),
              reads=[R("pCx")], writes=[rfl])
        for p_ in range(2):
            for half in range(2):
                kb.op("t", lambda e, p_=p_, half=half: e.matmul(
                    pC, onesf,
                    ROW6[0:1, p_ * 1024 + half * 512:p_ * 1024 + (half + 1) * 512],
                    start=True, stop=True),
                    reads=[rW12, R("const")], writes=[R("pCx")])
                kb.op("v", lambda e, p_=p_, half=half: e.tensor_copy(
                    WBC[p_][:, half * 512:(half + 1) * 512], pC),
                    reads=[R("pCx")], writes=[rWBC])
        rATP_ = R("attnp")
        rATP_.did_read(sV, sV.n)
        rATP_.did_read(sT, sT.n)

    if upto >= 7:
        act = Bb
        rACT = rATP
        rMOE = R("moesc")
        ACTT = [Df[:, 5120 + i * 1024:5120 + (i + 1) * 1024] for i in range(3)]
        AMRG = Df[:, 4096:5120]
        jcell = [jsel[0:1, 0:1], jsel[0:1, 1:2]]
        rjs = R("jsel")
        rbs = R("bsel")
        rjsw = rjs.w_waits()
        pass_done = []
        for p_ in range(2):
            fcol = lambda k, p_=p_: RW[:, 41 + p_ * 3 + k:42 + p_ * 3 + k]
            rfl = R("flg6")

            def w1_evac(si, pm, rpm, p_=p_, fcol=fcol):
                ra = R("acttmp")
                rmrg = R("amrg")
                for k, fn_ in enumerate((AF.Tanh, AF.Gelu, AF.Silu)):
                    kb.op("s", lambda e, pm=pm, k=k, fn_=fn_, si=si, p_=p_:
                          e.activation(ACTT[k], pm, fn_,
                                       bias=bsel[:, p_ * DT + si:
                                                 p_ * DT + si + 1]),
                          reads=[rpm, rbs], writes=[ra])
                kb.op("v", lambda e, fcol=fcol: e.tensor_scalar_mul(
                    AMRG, ACTT[0], fcol(0)), reads=[ra, rfl], writes=[rmrg])
                kb.op("v", lambda e, fcol=fcol: e.scalar_tensor_tensor(
                    AMRG, ACTT[1], fcol(1), AMRG, ALU.mult, ALU.add),
                    reads=[ra, rfl, rmrg], writes=[rmrg])
                kb.op("v", lambda e, fcol=fcol, si=si: e.scalar_tensor_tensor(
                    act[:, si * 1024:(si + 1) * 1024], ACTT[2], fcol(2), AMRG,
                    ALU.mult, ALU.add),
                    reads=[ra, rfl, rmrg], writes=[rACT])

            def emit_bsel(p_=p_):
                dma3("bsel", bsel[:, p_ * DT:(p_ + 1) * DT],
                     [b1_d[:, e_ * DT:(e_ + 1) * DT] for e_ in range(3)],
                     reads=[rjs], writes=[rbs])
                dma3("bsel", bsel[:, (2 + p_) * DT:(3 + p_) * DT],
                     [b2_d[:, e_ * DT:(e_ + 1) * DT] for e_ in range(3)],
                     reads=[rjs], writes=[rbs])

            def emit_w1(p_=p_, w1_evac=w1_evac):
                stream_mm_sw([w1_d[e_] for e_ in range(3)], DT, A, rHF, w1_evac,
                             W7,
                             first_extra=(R("X2T0").w_waits() +
                                          R("SQ5").w_waits())
                             if p_ == 0 else ())

            def w2_evac(si, pm, rpm, p_=p_):
                for tb in range(2):
                    stg = STG[(si * 2 + tb) % 4]
                    rstg = R(f"STG{(si * 2 + tb) % 4}")
                    kb.op("v", lambda e, pm=pm, tb=tb, stg=stg, si=si, p_=p_: (
                        e.scalar_tensor_tensor(
                            stg, pm[:, tb * 512:(tb + 1) * 512],
                            bsel[:, (2 + p_) * DT + si:(2 + p_) * DT + si + 1],
                            WBC[p_][:, tb * 512:(tb + 1) * 512],
                            ALU.add, ALU.mult)),
                        reads=[rpm, rbs, rWBC], writes=[rstg])
                    extra = pass_done[p_ - 1] if p_ > 0 else []
                    kw = {} if p_ == 0 else {"accum_op": ALU.add}
                    kb.dma("g", f"moe{(si * 2 + tb) % 4}",
                           moesc[si * 128:(si + 1) * 128, tb * 512:(tb + 1) * 512],
                           stg, reads=[rstg], writes=[rMOE], extra=extra, **kw)

            def emit_w2(w2_evac=w2_evac):
                stream_mm_sw([w2_d[e_] for e_ in range(3)], DT, act, rACT,
                             w2_evac, W7)

            def emit_pass(p_=p_, emit_bsel=emit_bsel, emit_w1=emit_w1,
                          emit_w2=emit_w2):
                emit_bsel()
                emit_w1()
                emit_w2()

            g_switch_region(jcell[p_], rjsw, emit_pass)
            pass_done.append([(kb.chan(f"moe{i}"), kb.chan(f"moe{i}").n)
                              for i in range(4)])

    if upto >= 8:
        rMOE = R("moesc")
        rOUT = R("outd")
        for tt in range(TT):
            ix = tt % 2
            MO8, MT8 = MO8x[ix], MT8x[ix]
            SQ8, NRM8 = SQ8x[ix], NRM8x[ix]
            rmo = R(f"MO8{ix}")
            mw = [(kb.chan(f"moe{i}"), kb.chan(f"moe{i}").n) for i in range(4)] \
                if tt == 0 else []
            kb.dma("y", f"mo8{ix}", MO8.rearrange("p (ct t) -> p ct t", t=128),
                   moesc.rearrange("(ct p) t -> p ct t", p=128)
                   [:, :, tt * 128:(tt + 1) * 128],
                   reads=[rMOE], writes=[rmo], extra=mw)
            rmt = R(f"MT8{ix}")
            for g4 in range(8):
                bank, rb = (pD, R("pD")) if g4 % 2 == 0 else (pC, R("pCx"))
                for k in range(4):
                    last = k == 3
                    kb.op("t", lambda e, b=bank, g4=g4, k=k, MO8=MO8: e.transpose(
                        b[:, k * 128:(k + 1) * 128],
                        MO8[:, (g4 * 4 + k) * 128:(g4 * 4 + k + 1) * 128], identf),
                        reads=[rmo, R("const")], writes=[rb], inc=last, record=last)
                kb.op("v", lambda e, b=bank, g4=g4, MT8=MT8: e.tensor_copy(
                    MT8[:, g4 * 512:(g4 + 1) * 512], b[:]),
                    reads=[rb], writes=[rmt])
            negm, rstd, r_c = ln_stats(rmt, MT8, SQ8, R(f"SQ8{ix}"),
                                       32 + 128 * ix)
            rn8 = R(f"NRM8{ix}")
            kb.op("v", lambda e, nm=negm, rs=rstd, MT8=MT8, NRM8=NRM8: e.tensor_scalar(
                NRM8, MT8, nm, rs, ALU.add, ALU.mult),
                reads=[rmt, r_c], writes=[rn8])
            rxr = R(f"XR8{tt % 2}")
            kb.dma("y", f"xr{tt % 2}", XR8[tt % 2], x2sc[tt * 128:(tt + 1) * 128, :],
                   reads=[rX2], writes=[rxr],
                   extra=(R("wb0").w_waits() + R("wb1").w_waits() +
                          R("wb2").w_waits() + R("wb3").w_waits())
                   if tt < 2 else ())
            kb.op("v", lambda e, tt=tt, MO8=MO8, NRM8=NRM8: e.tensor_tensor(
                MO8, XR8[tt % 2], NRM8, ALU.add),
                reads=[rxr, rn8], writes=[rmo])
            kb.dma("y", f"out{ix}", out_d[tt * 128:(tt + 1) * 128, :], MO8,
                   reads=[rmo], writes=[rOUT])
    else:
        rz = R("zero")
        kb.op("v", lambda e: e.memset(XT[0], 0.0), writes=[rz],
              extra=([(sT, sT.n), (sV, sV.n), (sA, sA.n), (sg, sg.n)] +
                     [(s, s.n) for s in kb.dsem.values() if s.n]))
        for tt in range(TT):
            kb.dma("y", "out", out_d[tt * 128:(tt + 1) * 128, :], XT[0], reads=[R("zero")])

    if tap is not None:
        arena = {"A": A, "Bb": Bb, "C": C, "Dd": Dd}[tap]
        tap_d = nc.dram_tensor("tap", [128, arena.shape[1]], BF16,
                               kind="ExternalOutput").ap()
        allw = [(s, s.n) for s in (sT, sV, sA, sg)]
        allw += [(s, s.n) for s in kb.dsem.values() if s.n]
        kb.dma("y", "out", tap_d, arena, extra=allw)
    def f_final(e):
        for s in kb.dsem.values():
            if s.n:
                e.wait_ge(s.h, s.n)
    kb.raw("y", f_final)
    kb.emit(blk)
    ctx.close()
    return nc



def _host_constants(half):
    T0 = half * TT
    pos = np.arange(half * T, (half + 1) * T, dtype=np.float64)
    inv = 10000.0 ** (-np.arange(64, dtype=np.float64) / 64.0)
    ang = inv[:, None] * pos[None, :]
    cosq = (np.cos(ang) * SM).astype(np.float32)
    sinq = (np.sin(ang) * SM).astype(np.float32)
    cosk = np.cos(ang).astype(np.float32)
    sink = np.sin(ang).astype(np.float32)
    pcol = np.arange(128)
    mo1 = np.zeros((128, TT, 1024), dtype=np.float32)
    mo1d = np.zeros((128, TT, 128), dtype=np.float32)
    for t in range(TT):
        ig = (T0 + t) * 128 + pcol
        jg = ((T0 + t - 8) * 128 + np.arange(1152))[None, :]
        ok = (jg >= 0) & (jg <= ig[:, None]) & (jg > ig[:, None] - WIN)
        m = np.where(ok, 0.0, NEG)
        mo1[:, t, :] = m[:, 0:1024]
        mo1d[:, t, :] = m[:, 1024:1152]
    return dict(
        cosq=cosq, sinq=sinq, cosk=cosk, sink=sink,
        mo1=np.ascontiguousarray(mo1.reshape(128, TT * 1024)).astype(BFNP),
        mo1d=np.ascontiguousarray(mo1d.reshape(128, TT * 128)).astype(BFNP),
        identf=np.eye(128, dtype=np.float32),
        onesb=np.ones((1, 128), dtype=BFNP),
        onesf=np.ones((1, 128), dtype=np.float32),
    )


def _prepare_inputs(inputs, nonce):
    x = np.ascontiguousarray(np.asarray(inputs["L0_attn_x"], dtype=np.float32))
    wq = np.ascontiguousarray(inputs["L0_attn_wq"])
    wk = np.ascontiguousarray(inputs["L0_attn_wk"])
    wv = np.ascontiguousarray(inputs["L0_attn_wv"])
    wo = np.ascontiguousarray(inputs["L0_attn_wo"])
    gw = np.asarray(inputs["L0_ffn_gate_w"])
    gb = np.asarray(inputs["L0_ffn_gate_b"])
    w1 = np.ascontiguousarray(inputs["L0_ffn_w1"])
    b1 = np.asarray(inputs["L0_ffn_b1"])
    w2 = np.ascontiguousarray(inputs["L0_ffn_w2"])
    b2 = np.asarray(inputs["L0_ffn_b2"])
    b1a = np.ascontiguousarray(
        b1.reshape(3, DT, 128).transpose(2, 0, 1).reshape(128, 3 * DT))
    b2a = np.ascontiguousarray(
        b2.reshape(3, DT, 128).transpose(2, 0, 1).reshape(128, 3 * DT))
    gwa = np.ascontiguousarray(
        gw[:D].reshape(DT, 128, 3).transpose(1, 0, 2).reshape(128, DT * 3))
    gwl = np.ascontiguousarray(gw[D:D + 1, :]).astype(np.float32)
    gba = np.ascontiguousarray(gb.reshape(1, 3)).astype(np.float32)
    in_maps = []
    for c in range(8):
        b_, h_ = c // 2, c % 2
        cst = _host_constants(h_)
        if h_ == 0:
            sel = np.array([[1, 0, 0, 1, 1, 1, 1, 0]], dtype=np.float32)
        else:
            sel = np.array([[0, 1, 1, 1, 0, 1, 0, 1]], dtype=np.float32)
        in_maps.append(dict(
            x_own=np.ascontiguousarray(x[b_, h_ * T:(h_ + 1) * T, :]),
            wq=wq, wk=wk, wv=wv, wo=wo, w1=w1, w2=w2,
            b1a=b1a, b2a=b2a, gwa=gwa, gwl=gwl, gb=gba,
            nonce=nonce, sel=sel, **cst))
    return in_maps


def kernel(**inputs):
    nonce = np.random.default_rng().integers(1, 2 ** 31 - 1, size=(1, 8),
                                             dtype=np.int32)
    in_maps = _prepare_inputs(inputs, nonce)
    if "full" not in _CACHE:
        _CACHE["full"] = build(upto=99)
    nc = _CACHE["full"]
    res = run_bass_kernel_spmd(nc, in_maps, core_ids=list(range(8)))
    out = np.empty((B, S, D), dtype=np.float32)
    for c in range(8):
        b_, h_ = c // 2, c % 2
        out[b_, h_ * T:(h_ + 1) * T, :] = res.results[c]["out"]
    return out
